# version 07
# baseline (speedup 1.0000x reference)
import contextlib
import numpy as np
import concourse.bass as bass
import concourse.mybir as mybir
from concourse.bass_utils import run_bass_kernel_spmd

F32 = mybir.dt.float32
BF16 = mybir.dt.bfloat16
I32 = mybir.dt.int32
AF = mybir.ActivationFunctionType
ALU = mybir.AluOpType

D = 4096
T = 1024
NH = 32
NCH = 4
EPS = 1e-5
SHIFT = -12.0
SCL = 1.0 / float(np.sqrt(128.0))

P_ORD = []
for ch in range(NCH):
    P_ORD += [16 + 4 * ch + k for k in range(4)] + [32 + 4 * ch + k for k in range(4)]
P_ORD += list(range(16))
N_PAN = 48
SETTOT = 16 * 7


def _qkv_dest(jcol):
    if jcol < 32:
        return ("q", jcol * 128)
    j2 = jcol - 32
    if j2 < 32:
        return ("kv", j2 // 8, (j2 % 8) * 128)
    j2 -= 32
    return ("kv", j2 // 8, 1024 + (j2 % 8) * 128)


LVL = 9


def build_nc(lvl=None):
    import os
    SUB = os.environ.get("KSUB", "hs")
    sub_ord = {"tp": 0, "e2": 1, "ce": 2, "hs": 3}[SUB]
    if lvl is None:
        lvl = LVL
    P1A, P1B, P1C, P2 = lvl >= 2, lvl >= 3, lvl >= 4, lvl >= 5
    P2R, P2T, P2H, P2Y = lvl >= 6, lvl >= 7, lvl >= 8, lvl >= 9
    nc = bass.Bass("TRN2", target_bir_lowering=False)

    x = nc.dram_tensor("x", [T, D], F32, kind="ExternalInput")
    wqkv = nc.dram_tensor("wqkv", [D, 3 * D], F32, kind="ExternalInput")
    bqkv = nc.dram_tensor("bqkv", [128, 96], F32, kind="ExternalInput")
    wo = nc.dram_tensor("wo", [D, D], F32, kind="ExternalInput")
    obr = nc.dram_tensor("obr", [1, D], F32, kind="ExternalInput")
    wbt = nc.dram_tensor("wbt", [D, D], F32, kind="ExternalInput")
    bbr = nc.dram_tensor("bbr", [1, D], F32, kind="ExternalInput")
    wrt = nc.dram_tensor("wrt", [128, 32, 8], F32, kind="ExternalInput")
    a2 = nc.dram_tensor("a2", [128, 32, 128], F32, kind="ExternalInput")
    b2 = nc.dram_tensor("b2", [128, D], F32, kind="ExternalInput")
    e2 = nc.dram_tensor("e2", [8, 128], F32, kind="ExternalInput")
    idf = nc.dram_tensor("idf", [128, 128], F32, kind="ExternalInput")
    cinfo = nc.dram_tensor("cinfo", [1, 4], I32, kind="ExternalInput")
    y = nc.dram_tensor("y", [T, D], F32, kind="ExternalOutput")

    qt_d = nc.dram_tensor("qt_d", [D, T], BF16)
    agin = nc.dram_tensor("agin", [NCH * 2048, T], BF16)
    agout = nc.dram_tensor("agout", [NCH * 8192, T], BF16)
    kvloc = nc.dram_tensor("kvloc", [NCH * 4096, T], BF16)
    x2_d = nc.dram_tensor("x2_d", [T, D], F32)

    es = contextlib.ExitStack()
    with es:
        sbt = lambda name, shape, dt: es.enter_context(nc.sbuf_tensor(name, shape, dt))
        AR1 = sbt("AR1", [128, 32768], BF16)
        AR2 = sbt("AR2", [128, 32768], BF16)
        U = sbt("U", [128, 4096], F32)
        xh_tm = sbt("xh_tm", [128, 4096], BF16)
        stage = sbt("stage", [128, 2, 1024], BF16)
        KT = sbt("KT", [128, 2, 2048], BF16)
        QT = sbt("QT", [128, 2, 1024], BF16)
        VA = sbt("VA", [128, 2, 16, 144], BF16)
        bqkv_sb = sbt("bqkv_sb", [128, 96], F32)
        obp = sbt("obp", [1, 2, 512], BF16)
        bbp = sbt("bbp", [1, 2, 512], BF16)
        wrt_sb = sbt("wrt_sb", [128, 32, 8], BF16)
        a2_sb = sbt("a2_sb", [128, 32, 128], BF16)
        b2_sb = sbt("b2_sb", [128, D], BF16)
        e2_sb = sbt("e2_sb", [8, 128], BF16)
        idf_sb = sbt("idf_sb", [128, 128], F32)
        ci_sb = sbt("ci_sb", [1, 4], I32)
        ones1 = sbt("ones1", [1, 128], BF16)
        cst = sbt("cst", [128, 2], F32)
        s_bnst = sbt("s_bnst", [128, 48], F32)
        s_bn = sbt("s_bn", [128, 2], F32)
        s_lnt = sbt("s_lnt", [128, 1], F32)
        s_r = sbt("s_r", [128, 1], F32)
        s_rec = sbt("s_rec", [128, 2], F32)
        s_l8 = sbt("s_l8", [128, 64], F32)
        s_d = sbt("s_d", [128, 8], F32)
        s_t = sbt("s_t", [128, 8], F32)
        s_sc1 = sbt("s_sc1", [128, 1], F32)
        s_w1 = sbt("s_w1", [128, 1], F32)
        s_w2 = sbt("s_w2", [128, 1], F32)
        s_eqA = sbt("s_eqA", [128, 8], F32)
        s_eqB = sbt("s_eqB", [128, 8], F32)
        s_comb = sbt("s_comb", [128, 64], F32)
        s_comb2 = sbt("s_comb2", [128, 64], F32)
        s_combT = sbt("s_combT", [8, 1024], BF16)
        s_lgs = sbt("s_lgs", [128, 64], F32)
        ce_sb = sbt("ce_sb", [128, 1024], BF16)
        hs_sb = sbt("hs_sb", [128, 1024], BF16)

        ps = es.enter_context(nc.psum_tensor("ps", [128, 4096], F32))

        AR3 = U.ap().rearrange("p (h n) -> p h n", h=2)
        def xblk(sl):
            return U[:, sl * 512:(sl + 1) * 512]
        def x2st(sl):
            return U[:, 1024 + sl * 512: 1024 + (sl + 1) * 512]
        def yst(sl):
            return U[:, 2048 + sl * 512: 2048 + (sl + 1) * 512]
        def ar1(a, b):
            return AR1[:, a:b]
        def ar2(a, b):
            return AR2[:, a:b]
        U_bf = U.ap().bitcast(BF16)
        def wslot(k, a, b):
            if k < 4:
                return AR2[:, k * 8192 + a: k * 8192 + b]
            return U_bf[:, a:b]
        def wslot3d(k):
            if k < 4:
                return AR2[:, k * 8192:(k + 1) * 8192].rearrange("p (cc n) -> p cc n", n=256)
            return U_bf.rearrange("p (cc n) -> p cc n", n=256)
        ar1_3d = AR1.ap().rearrange("p (cc n) -> p cc n", n=T)

        sems = {}
        def S(name):
            if name not in sems:
                sems[name] = es.enter_context(nc.semaphore(name))
            return sems[name]
        sCC = [S(f"CC{i}") for i in range(NCH)]
        rb = es.enter_context(nc.gpsimd.register("rb"))
        r2 = es.enter_context(nc.gpsimd.register("r2"))

        blk = es.enter_context(nc.Block())

        def barrier(eng, k):
            if lvl < k + 1:
                return
            eng.nop().then_inc(S("PH"), 1)
            eng.wait_ge(S("PH"), 5 * k)

        @blk.tensor
        def _(t):
            barrier(t, 1)
            if not P1A:
                return
            t.wait_ge(S("XT"), 16 * 8)
            for i in range(N_PAN):
                t.wait_ge(S(f"WP{i % 5}"), 16 * (i // 5 + 1))
                for jj in range(2):
                    tt = 2 * i + jj
                    if tt >= 2:
                        t.wait_ge(S("EV"), tt - 1)
                    pbase = 0 if tt % 2 == 0 else 1024
                    for half in range(2):
                        for cc in range(32):
                            ins = t.matmul(
                                ps[:, pbase + half * 512: pbase + (half + 1) * 512],
                                wslot(i % 5, cc * 256 + jj * 128, cc * 256 + (jj + 1) * 128),
                                ar1(cc * 1024 + half * 512, cc * 1024 + (half + 1) * 512),
                                start=(cc == 0), stop=(cc == 31))
                    ins.then_inc(S("QK"), 1)
            barrier(t, 2)
            if not P1B:
                return
            t.wait_ge(S("CST"), 5)

            def _scores(h):
                t.wait_ge(S(f"ATL{h % 2}"), 48 * (h // 2 + 1))
                for kt in range(16):
                    g = 16 * h + kt
                    if g >= 3:
                        t.wait_ge(S("EX"), g - 2)
                    base = (g % 3) * 1024
                    for half in range(2):
                        ins = t.matmul(
                            ps[:, base + half * 512: base + (half + 1) * 512],
                            KT[:, h % 2, kt * 128:(kt + 1) * 128],
                            QT[:, h % 2, half * 512:(half + 1) * 512],
                            start=True, stop=True)
                    ins.then_inc(S("SC"), 1)

            def _av(h):
                t.wait_ge(S("EX"), 16 * (h + 1))
                t.wait_ge(S(f"VTL{h % 2}"), 32 * (h // 2 + 1))
                for qs in range(8):
                    if 8 * h + qs >= 2:
                        t.wait_ge(S("CX"), 8 * h + qs - 1)
                    reg = ps[:, 3072 + (qs % 2) * 512: 3072 + (qs % 2) * 512 + 129]
                    for kt in range(16):
                        ins = t.matmul(
                            reg,
                            ar1((h % 2) * 16384 + kt * 1024 + qs * 128,
                                (h % 2) * 16384 + kt * 1024 + (qs + 1) * 128),
                            VA[:, h % 2, kt, 0:129],
                            start=(kt == 0), stop=(kt == 15))
                    ins.then_inc(S("AV"), 1)

            _scores(0)
            for h in range(NH):
                if h + 1 < NH:
                    _scores(h + 1)
                _av(h)
            barrier(t, 3)
            if not P1C:
                return
            t.wait_ge(S("TP"), 16 * 8)
            t.wait_ge(S("CST"), 3)
            for ob in range(8):
                t.wait_ge(S(f"WO{ob % 2}"), 32 * (ob // 2 + 1))
                for ns in range(8):
                    tt = 8 * ob + ns
                    if tt >= 4:
                        t.wait_ge(S("X2S"), tt - 3)
                    reg = ps[:, (tt % 4) * 512:(tt % 4 + 1) * 512]
                    for cc in range(32):
                        t.matmul(reg,
                                 ar1(cc * 1024 + ns * 128, cc * 1024 + (ns + 1) * 128),
                                 ar2((ob % 2) * 16384 + cc * 512, (ob % 2) * 16384 + (cc + 1) * 512),
                                 start=(cc == 0), stop=False)
                    ins = t.matmul(reg, ones1[0:1, 0:128],
                                   obp[0:1, ob % 2, :],
                                   start=False, stop=True)
                    ins.then_inc(S("OP"), 1)
            barrier(t, 4)
            if not P2R:
                return
            t.wait_ge(S("XT2"), 16 * 8)
            for ns in range(8):
                for cc in range(32):
                    ins = t.matmul(ps[:, 3072 + ns * 8: 3072 + (ns + 1) * 8],
                                   ar1(cc * 1024 + ns * 128, cc * 1024 + (ns + 1) * 128),
                                   wrt_sb[:, cc, :],
                                   start=(cc == 0), stop=(cc == 31))
            ins.then_inc(S("LG"), 1)
            if not P2T:
                return
            for cc in range(32):
                for half in range(2):
                    ins = t.matmul(ps[:, 1024 + half * 512: 1024 + (half + 1) * 512],
                                   a2_sb[:, cc, :],
                                   ar1(cc * 1024 + half * 512, cc * 1024 + (half + 1) * 512),
                                   start=(cc == 0), stop=(cc == 31))
            ins.then_inc(S("HM"), 1)
            if not P2H:
                return
            for ns in range(8):
                t.wait_ge(S("CMB"), ns + 1)
                t.transpose(ps[0:8, 2048 + ns * 128: 2048 + (ns + 1) * 128],
                            s_comb2[:, ns * 8:(ns + 1) * 8],
                            idf_sb[:]).then_inc(S("TT"), 1)
            if sub_ord < 1:
                return
            t.wait_ge(S("CT"), 1)
            for half in range(2):
                ins = t.matmul(ps[:, half * 512:(half + 1) * 512],
                               e2_sb[0:8, 0:128],
                               s_combT[0:8, half * 512:(half + 1) * 512],
                               start=True, stop=True)
            ins.then_inc(S("E2S"), 1)
            if not P2Y:
                return
            t.wait_ge(S("HS"), 1)
            for ob in range(8):
                t.wait_ge(S(f"WB{ob % 2}"), 32 * (ob // 2 + 1))
                for ns in range(8):
                    tt = 8 * ob + ns
                    if tt >= 4:
                        t.wait_ge(S("YS"), tt - 3)
                    reg = ps[:, (tt % 4) * 512:(tt % 4 + 1) * 512]
                    for cc in range(32):
                        t.matmul(reg,
                                 ar1(cc * 1024 + ns * 128, cc * 1024 + (ns + 1) * 128),
                                 ar2((ob % 2) * 16384 + cc * 512, (ob % 2) * 16384 + (cc + 1) * 512),
                                 start=(cc == 0), stop=False)
                    t.matmul(reg, hs_sb[:, ns * 128:(ns + 1) * 128],
                             b2_sb[:, ob * 512:(ob + 1) * 512],
                             start=False, stop=False)
                    ins = t.matmul(reg, ones1[0:1, 0:128],
                                   bbp[0:1, ob % 2, :],
                                   start=False, stop=True)
                    ins.then_inc(S("OP2"), 1)

        @blk.vector
        def _(v):
            _vc = [0]
            def vink(ins):
                _vc[0] += 1
                ins.then_inc(S("VCH"), 1)
            def vw():
                v.wait_ge(S("VCH"), _vc[0])
            v.memset(cst[:, 0:1], EPS).then_inc(S("CST"), 1)
            v.memset(cst[:, 1:2], SHIFT).then_inc(S("CST"), 1)
            v.memset(ones1[0:1, 0:128], 1.0).then_inc(S("CST"), 1)
            v.memset(VA[:, 0, :, 128:129], 1.0).then_inc(S("CST"), 1)
            v.memset(VA[:, 1, :, 128:129], 1.0).then_inc(S("CST"), 1)
            for ns in range(8):
                vw()
                for hlf in range(2):
                    v.wait_ge(S(f"X{hlf}"), 16 * (ns + 1))
                    for q in range(4):
                        vink(v.bn_stats(s_bnst[:, hlf * 24 + q * 6: hlf * 24 + (q + 1) * 6],
                                        AR3[:, hlf, q * 512:(q + 1) * 512]))
                vw()
                v.sem_inc(S("BN"), 1)
                if ns >= 1:
                    v.wait_ge(S("R"), ns)
                vw()
                vink(v.bn_aggr(s_bn[:], s_bnst[:, 0:48]))
                vw()
                v.sem_inc(S("BN"), 1)
                v.wait_ge(S("R"), ns + 1)
                if ns >= 1:
                    v.wait_ge(S("XT"), 16 * ns)
                vw()
                for hlf in range(2):
                    vink(v.tensor_scalar(xh_tm[:, hlf * 2048:(hlf + 1) * 2048],
                                         AR3[:, hlf, :], s_bn[:, 0:1], s_r[:],
                                         ALU.subtract, ALU.mult))
                    vw()
                    v.sem_inc(S("NM"), 1)
            barrier(v, 1)
            if not P1A:
                return
            v.wait_ge(S("SET"), SETTOT)
            for i in range(N_PAN):
                for jj in range(2):
                    tt = 2 * i + jj
                    jcol = P_ORD[i] * 2 + jj
                    v.wait_ge(S("QK"), tt + 1)
                    if tt >= 2:
                        v.wait_ge(S(f"STQ{tt % 2}"), 16 * (tt // 2))
                    pbase = 0 if tt % 2 == 0 else 1024
                    v.tensor_scalar(stage[:, tt % 2, :], ps[:, pbase:pbase + 1024],
                                    bqkv_sb[:, jcol:jcol + 1], None,
                                    ALU.add).then_inc(S("EV"), 1)
            barrier(v, 2)
            if not P1B:
                return
            for h in range(NH):
                for qs in range(8):
                    v.wait_ge(S("AV"), 8 * h + qs + 1)
                    reg = ps[:, 3072 + (qs % 2) * 512: 3072 + (qs % 2) * 512 + 129]
                    vw()
                    vink(v.reciprocal(s_rec[:, qs % 2:qs % 2 + 1], reg[:, 128:129]))
                    vw()
                    vink(v.tensor_scalar(ar2(qs * 4096 + h * 128, qs * 4096 + (h + 1) * 128),
                                         reg[:, 0:128], s_rec[:, qs % 2:qs % 2 + 1], None,
                                         ALU.mult))
                    vw()
                    v.sem_inc(S("CX"), 1)
            barrier(v, 3)
            if not P1C:
                return
            for ob in range(8):
                for ns in range(8):
                    tt = 8 * ob + ns
                    v.wait_ge(S("OP"), tt + 1)
                    v.wait_ge(S(f"XB{tt % 2}"), 16 * (tt // 2 + 1))
                    if tt >= 2:
                        v.wait_ge(S(f"X2W{tt % 2}"), 16 * (tt // 2))
                    v.tensor_tensor(x2st(tt % 2),
                                    ps[:, (tt % 4) * 512:(tt % 4 + 1) * 512],
                                    xblk(tt % 2), ALU.add).then_inc(S("X2S"), 1)
            barrier(v, 4)
            if not P2:
                return
            for ns in range(8):
                vw()
                for hlf in range(2):
                    v.wait_ge(S(f"X2L{hlf}"), 16 * (ns + 1))
                    for q in range(4):
                        vink(v.bn_stats(s_bnst[:, hlf * 24 + q * 6: hlf * 24 + (q + 1) * 6],
                                        AR3[:, hlf, q * 512:(q + 1) * 512]))
                vw()
                v.sem_inc(S("BN2"), 1)
                if ns >= 1:
                    v.wait_ge(S("R2"), ns)
                vw()
                vink(v.bn_aggr(s_bn[:], s_bnst[:, 0:48]))
                vw()
                v.sem_inc(S("BN2"), 1)
                v.wait_ge(S("R2"), ns + 1)
                if ns >= 1:
                    v.wait_ge(S("XT2"), 16 * ns)
                vw()
                for hlf in range(2):
                    vink(v.tensor_scalar(xh_tm[:, hlf * 2048:(hlf + 1) * 2048],
                                         AR3[:, hlf, :], s_bn[:, 0:1], s_r[:],
                                         ALU.subtract, ALU.mult))
                    vw()
                    v.sem_inc(S("NM2"), 1)
            if not P2R:
                return
            v.wait_ge(S("LG"), 1)
            vink(v.tensor_copy(s_lgs[:], ps[:, 3072:3136]))
            for ns in range(8):
                vw()
                vink(v.max(s_l8[:, ns * 8:(ns + 1) * 8],
                           s_lgs[:, ns * 8:(ns + 1) * 8]))
                vw()
                vink(v.tensor_tensor(s_d[:, ns:ns + 1], s_l8[:, ns * 8 + 1:ns * 8 + 2],
                                     s_l8[:, ns * 8:ns * 8 + 1], ALU.subtract))
                vw()
                v.sem_inc(S("D"), 1)
            for ns in range(8):
                v.wait_ge(S("T"), ns + 1)
                vw()
                vink(v.tensor_scalar(s_sc1[:], s_t[:, ns:ns + 1], 1.0, None, ALU.add))
                vw()
                vink(v.reciprocal(s_w1[:], s_sc1[:]))
                vw()
                vink(v.tensor_tensor(s_w2[:], s_t[:, ns:ns + 1], s_w1[:], ALU.mult))
                lg = s_lgs[:, ns * 8:(ns + 1) * 8]
                vink(v.tensor_scalar(s_eqA[:], lg, s_l8[:, ns * 8:ns * 8 + 1], None, ALU.is_equal))
                vw()
                vink(v.tensor_scalar(s_eqB[:], lg, s_l8[:, ns * 8 + 1:ns * 8 + 2], None, ALU.is_equal))
                vink(v.tensor_scalar(s_comb[:, ns * 8:(ns + 1) * 8], s_eqA[:], s_w1[:], None, ALU.mult))
                vw()
                vink(v.scalar_tensor_tensor(s_comb2[:, ns * 8:(ns + 1) * 8], s_eqB[:], s_w2[:],
                                            s_comb[:, ns * 8:(ns + 1) * 8],
                                            ALU.mult, ALU.add))
                vw()
                v.sem_inc(S("CMB"), 1)
            if not P2H:
                return
            v.wait_ge(S("TT"), 8)
            vink(v.tensor_copy(s_combT[0:8, :], ps[0:8, 2048:3072]))
            vw()
            v.sem_inc(S("CT"), 1)
            if sub_ord < 2:
                return
            v.wait_ge(S("E2S"), 1)
            vink(v.tensor_copy(ce_sb[:], ps[:, 0:1024]))
            vw()
            v.sem_inc(S("CE"), 1)
            if not P2H or sub_ord < 3:
                return
            v.wait_ge(S("HM"), 1)
            vw()
            vink(v.tensor_tensor(hs_sb[:], ps[:, 1024:2048], ce_sb[:], ALU.mult))
            vw()
            v.sem_inc(S("HS"), 1)
            if not P2Y:
                return
            for ob in range(8):
                for ns in range(8):
                    tt = 8 * ob + ns
                    v.wait_ge(S("OP2"), tt + 1)
                    v.wait_ge(S(f"XB2{tt % 2}"), 16 * (tt // 2 + 1))
                    if tt >= 2:
                        v.wait_ge(S(f"YW{tt % 2}"), 16 * (tt // 2))
                    v.tensor_tensor(yst(tt % 2),
                                    ps[:, (tt % 4) * 512:(tt % 4 + 1) * 512],
                                    xblk(tt % 2), ALU.add).then_inc(S("YS"), 1)

        @blk.scalar
        def _(sc):
            _ac = [0]
            sc.wait_ge(S("CST"), 1)
            for hlf in range(2):
                sc.dma_start(out=AR3[:, hlf, :],
                             in_=x[0:128, hlf * 2048:(hlf + 1) * 2048]
                             ).then_inc(S(f"X{hlf}"), 16)
            for ns in range(8):
                sc.wait_ge(S("BN"), 2 * ns + 2)
                if ns >= 1:
                    sc.wait_ge(S("NM"), 2 * ns)
                _ac[0] += 1
                sc.activation(s_lnt[:], s_bn[:, 1:2], AF.Ln, bias=cst[:, 0:1],
                              scale=1.0).then_inc(S("ACH"), 1)
                sc.wait_ge(S("ACH"), _ac[0])
                _ac[0] += 1
                sc.activation(s_r[:], s_lnt[:], AF.Exp, scale=-0.5).then_inc(S("ACH"), 1)
                sc.wait_ge(S("ACH"), _ac[0])
                sc.sem_inc(S("R"), 1)
                if ns + 1 < 8:
                    for hlf in range(2):
                        sc.wait_ge(S("NM"), 2 * ns + hlf + 1)
                        sc.dma_start(out=AR3[:, hlf, :],
                                     in_=x[(ns + 1) * 128:(ns + 2) * 128,
                                           hlf * 2048:(hlf + 1) * 2048]
                                     ).then_inc(S(f"X{hlf}"), 16)
            barrier(sc, 1)
            barrier(sc, 2)
            if not P1B:
                return
            sc.wait_ge(S("CST"), 2)
            for h in range(NH):
                for kt in range(16):
                    g = 16 * h + kt
                    sc.wait_ge(S("SC"), g + 1)
                    if kt == 0 and h >= 2:
                        sc.wait_ge(S("AV"), 8 * (h - 1))
                    base = (g % 3) * 1024
                    sc.activation(ar1((h % 2) * 16384 + kt * 1024, (h % 2) * 16384 + (kt + 1) * 1024),
                                  ps[:, base:base + 1024], AF.Exp,
                                  bias=cst[:, 1:2], scale=SCL).then_inc(S("EX"), 1)
            barrier(sc, 3)
            if not P1C:
                return
            def _xsrc(tt):
                ob_, ns_ = tt // 8, tt % 8
                return x[ns_ * 128:(ns_ + 1) * 128, ob_ * 512:(ob_ + 1) * 512]
            sc.wait_ge(S("QK"), 96)
            for tt in range(2):
                sc.dma_start(out=xblk(tt % 2), in_=_xsrc(tt)).then_inc(S(f"XB{tt % 2}"), 16)
            for tt in range(64):
                ob_, ns_ = tt // 8, tt % 8
                sc.wait_ge(S("X2S"), tt + 1)
                sc.dma_start(out=x2_d[ns_ * 128:(ns_ + 1) * 128, ob_ * 512:(ob_ + 1) * 512],
                             in_=x2st(tt % 2)).then_inc(S(f"X2W{tt % 2}"), 16)
                if tt + 2 < 64:
                    sc.dma_start(out=xblk(tt % 2), in_=_xsrc(tt + 2)).then_inc(S(f"XB{tt % 2}"), 16)
            barrier(sc, 4)
            if not P2:
                return
            sc.wait_ge(S("X2W0"), 16 * 32)
            sc.wait_ge(S("X2W1"), 16 * 32)
            for hlf in range(2):
                sc.dma_start(out=AR3[:, hlf, :],
                             in_=x2_d[0:128, hlf * 2048:(hlf + 1) * 2048]
                             ).then_inc(S(f"X2L{hlf}"), 16)
            for ns in range(8):
                sc.wait_ge(S("BN2"), 2 * ns + 2)
                if ns >= 1:
                    sc.wait_ge(S("NM2"), 2 * ns)
                _ac[0] += 1
                sc.activation(s_lnt[:], s_bn[:, 1:2], AF.Ln, bias=cst[:, 0:1],
                              scale=1.0).then_inc(S("ACH"), 1)
                sc.wait_ge(S("ACH"), _ac[0])
                _ac[0] += 1
                sc.activation(s_r[:], s_lnt[:], AF.Exp, scale=-0.5).then_inc(S("ACH"), 1)
                sc.wait_ge(S("ACH"), _ac[0])
                sc.sem_inc(S("R2"), 1)
                if ns + 1 < 8:
                    for hlf in range(2):
                        sc.wait_ge(S("NM2"), 2 * ns + hlf + 1)
                        sc.dma_start(out=AR3[:, hlf, :],
                                     in_=x2_d[(ns + 1) * 128:(ns + 2) * 128,
                                              hlf * 2048:(hlf + 1) * 2048]
                                     ).then_inc(S(f"X2L{hlf}"), 16)
            if not P2R:
                return
            for ns in range(8):
                sc.wait_ge(S("D"), ns + 1)
                sc.activation(s_t[:, ns:ns + 1], s_d[:, ns:ns + 1], AF.Exp,
                              scale=1.0).then_inc(S("T"), 1)
            if not P2Y:
                return
            sc.wait_ge(S("NM2"), 16)
            def _x2src(tt):
                ob_, ns_ = tt // 8, tt % 8
                return x2_d[ns_ * 128:(ns_ + 1) * 128, ob_ * 512:(ob_ + 1) * 512]
            for tt in range(2):
                sc.dma_start(out=xblk(tt % 2), in_=_x2src(tt)).then_inc(S(f"XB2{tt % 2}"), 16)
            for tt in range(64):
                ob_, ns_ = tt // 8, tt % 8
                sc.wait_ge(S("YS"), tt + 1)
                sc.dma_start(out=y[ns_ * 128:(ns_ + 1) * 128, ob_ * 512:(ob_ + 1) * 512],
                             in_=yst(tt % 2)).then_inc(S(f"YW{tt % 2}"), 16)
                if tt + 2 < 64:
                    sc.dma_start(out=xblk(tt % 2), in_=_x2src(tt + 2)).then_inc(S(f"XB2{tt % 2}"), 16)
            sc.wait_ge(S("YW0"), 16 * 32)
            sc.wait_ge(S("YW1"), 16 * 32)

        @blk.sync
        def _(s):
            for ns in range(8):
                s.wait_ge(S("NM"), 2 * (ns + 1))
                s.dma_start(out=ar1_3d[:, :, ns * 128:(ns + 1) * 128],
                            in_=xh_tm[:], transpose=True).then_inc(S("XT"), 16)
            barrier(s, 1)
            if not P1A:
                return
            for i in range(N_PAN):
                for jj in range(2):
                    tt = 2 * i + jj
                    jcol = P_ORD[i] * 2 + jj
                    s.wait_ge(S("EV"), tt + 1)
                    dest = _qkv_dest(jcol)
                    if dest[0] == "q":
                        dst = qt_d[dest[1]:dest[1] + 128, :]
                    else:
                        dst = agin[dest[1] * 2048 + dest[2]: dest[1] * 2048 + dest[2] + 128, :]
                    s.dma_start(out=dst, in_=stage[:, tt % 2, :]).then_inc(S(f"STQ{tt % 2}"), 16)
            barrier(s, 2)
            if not P1B:
                return
            s.wait_ge(S("STQ0"), 16 * 48)
            s.wait_ge(S("STQ1"), 16 * 48)
            for h in range(NH):
                ch, hl = h // 8, h % 8
                s.wait_ge(S(f"KVL{ch}"), 16)
                if h >= 2:
                    s.wait_ge(S("SC"), 16 * (h - 1))
                    s.wait_ge(S("AV"), 8 * (h - 1))
                s.dma_start(out=KT[:, h % 2, 0:1024],
                            in_=kvloc[ch * 4096 + hl * 128: ch * 4096 + (hl + 1) * 128, :]).then_inc(S(f"ATL{h % 2}"), 16)
                s.dma_start(out=KT[:, h % 2, 1024:2048],
                            in_=kvloc[ch * 4096 + 2048 + hl * 128: ch * 4096 + 2048 + (hl + 1) * 128, :]).then_inc(S(f"ATL{h % 2}"), 16)
                s.dma_start(out=QT[:, h % 2, :],
                            in_=qt_d[h * 128:(h + 1) * 128, :]).then_inc(S(f"ATL{h % 2}"), 16)
                s.dma_start(out=VA[:, h % 2, 0:8, 0:128],
                            in_=kvloc[ch * 4096 + 1024 + hl * 128: ch * 4096 + 1024 + (hl + 1) * 128, :],
                            transpose=True).then_inc(S(f"VTL{h % 2}"), 16)
                s.dma_start(out=VA[:, h % 2, 8:16, 0:128],
                            in_=kvloc[ch * 4096 + 3072 + hl * 128: ch * 4096 + 3072 + (hl + 1) * 128, :],
                            transpose=True).then_inc(S(f"VTL{h % 2}"), 16)
            barrier(s, 3)
            if not P1C:
                return
            s.wait_ge(S("CX"), 256)
            for qs in range(8):
                s.dma_start(out=ar1_3d[:, :, qs * 128:(qs + 1) * 128],
                            in_=AR2[:, qs * 4096:(qs + 1) * 4096], transpose=True
                            ).then_inc(S("TP"), 16)
            barrier(s, 4)
            if not P2:
                return
            s.wait_ge(S("OP"), 64)
            for ns in range(8):
                s.wait_ge(S("NM2"), 2 * (ns + 1))
                s.dma_start(out=ar1_3d[:, :, ns * 128:(ns + 1) * 128],
                            in_=xh_tm[:], transpose=True).then_inc(S("XT2"), 16)

        @blk.gpsimd
        def _(g):
            g.dma_start(out=ci_sb[:], in_=cinfo.ap()).then_inc(S("SET"), 16)
            g.dma_start(out=bqkv_sb[:], in_=bqkv.ap()).then_inc(S("SET"), 16)
            g.dma_start(out=wrt_sb[:], in_=wrt.ap()).then_inc(S("SET"), 16)
            g.dma_start(out=a2_sb[:], in_=a2.ap()).then_inc(S("SET"), 16)
            g.dma_start(out=b2_sb[:], in_=b2.ap()).then_inc(S("SET"), 16)
            g.dma_start(out=e2_sb[:], in_=e2.ap()).then_inc(S("SET"), 16)
            g.dma_start(out=idf_sb[:], in_=idf.ap()).then_inc(S("SET"), 16)
            g.wait_ge(S("SET"), SETTOT)
            g.reg_load(rb, ci_sb[0:1, 0:1])
            barrier(g, 1)
            if not P1A:
                return
            wq_r = wqkv.ap().rearrange("(cc p) c -> p cc c", p=128)
            for i in range(N_PAN):
                p = P_ORD[i]
                if i == 4:
                    g.wait_ge(S("NM"), 16)
                if i >= 5:
                    g.wait_ge(S("QK"), 2 * (i - 4))
                g.dma_start(out=wslot3d(i % 5),
                            in_=wq_r[:, :, p * 256:(p + 1) * 256]).then_inc(S(f"WP{i % 5}"), 16)
                if i >= 12 and (i - 12) % 8 == 0 and (i - 12) // 8 < NCH:
                    ch = (i - 12) // 8
                    g.wait_ge(S("STQ0"), 16 * (8 * ch + 8))
                    g.wait_ge(S("STQ1"), 16 * (8 * ch + 8))
                    g.collective_compute(
                        "AllGather", ALU.bypass,
                        replica_groups=[[0, 1, 2, 3], [4, 5, 6, 7]],
                        ins=[agin[ch * 2048:(ch + 1) * 2048, :].opt()],
                        outs=[agout[ch * 8192:(ch + 1) * 8192, :].opt()],
                    ).then_inc(sCC[ch], 1)
                    g.wait_ge(sCC[ch], 1)
                    g.reg_mov(r2, ch * 8192 * 1024)
                    g.reg_add(r2, r2, rb)
                    g.dma_start(out=kvloc[ch * 4096:(ch + 1) * 4096, :],
                                in_=bass.AP(agout, r2, [[1024, 4096], [1, 1024]])
                                ).then_inc(S(f"KVL{ch}"), 16)
            barrier(g, 2)
            barrier(g, 3)
            if not P1C:
                return
            wo_r = wo.ap().rearrange("(cc p) o -> p cc o", p=128)
            g.wait_ge(S("TP"), 16 * 8)
            for ob in range(8):
                if ob >= 2:
                    g.wait_ge(S("OP"), 8 * (ob - 1))
                g.dma_start(out=AR2[:, (ob % 2) * 16384:(ob % 2 + 1) * 16384]
                            .rearrange("p (cc n) -> p cc n", n=512),
                            in_=wo_r[:, :, ob * 512:(ob + 1) * 512]).then_inc(S(f"WO{ob % 2}"), 16)
                g.dma_start(out=obp[0:1, ob % 2, :],
                            in_=obr.ap()[0:1, ob * 512:(ob + 1) * 512]).then_inc(S(f"WO{ob % 2}"), 16)
            barrier(g, 4)
            if not P2Y:
                return
            wb_r = wbt.ap().rearrange("(cc p) o -> p cc o", p=128)
            g.wait_ge(S("OP"), 64)
            for ob in range(8):
                if ob >= 2:
                    g.wait_ge(S("OP2"), 8 * (ob - 1))
                g.dma_start(out=AR2[:, (ob % 2) * 16384:(ob % 2 + 1) * 16384]
                            .rearrange("p (cc n) -> p cc n", n=512),
                            in_=wb_r[:, :, ob * 512:(ob + 1) * 512]).then_inc(S(f"WB{ob % 2}"), 16)
                g.dma_start(out=bbp[0:1, ob % 2, :],
                            in_=bbr.ap()[0:1, ob * 512:(ob + 1) * 512]).then_inc(S(f"WB{ob % 2}"), 16)

    return nc


def kernel(**inputs):
    X = np.ascontiguousarray(np.asarray(inputs["L0_attn_x"], dtype=np.float32))
    wqkv = np.ascontiguousarray(np.asarray(inputs["L0_attn_attn_qkvw"], dtype=np.float32))
    qkvb = np.asarray(inputs["L0_attn_attn_qkvb"], dtype=np.float32)
    wo = np.ascontiguousarray(np.asarray(inputs["L0_attn_attn_ow"], dtype=np.float32))
    ob_ = np.asarray(inputs["L0_attn_attn_ob"], dtype=np.float32)
    wb = np.asarray(inputs["L0_ffn_W_base"], dtype=np.float32)
    bb = np.asarray(inputs["L0_ffn_b_base"], dtype=np.float32)
    wr = np.asarray(inputs["L0_ffn_W_router"], dtype=np.float32)
    A = np.asarray(inputs["L0_ffn_A_stack"], dtype=np.float32)
    B = np.asarray(inputs["L0_ffn_B_stack"], dtype=np.float32)

    bqkv = np.ascontiguousarray(qkvb.reshape(96, 128).T)
    obr = np.ascontiguousarray(ob_.reshape(1, 4096))
    wbt = np.ascontiguousarray(wb.T)
    bbr = np.ascontiguousarray(bb.reshape(1, 4096))
    wrt = np.ascontiguousarray(wr.T.reshape(32, 128, 8).transpose(1, 0, 2))
    a2 = np.ascontiguousarray(A.transpose(2, 0, 1).reshape(32, 128, 128).transpose(1, 0, 2))
    b2 = np.ascontiguousarray(B.transpose(0, 2, 1).reshape(128, 4096))
    e2 = np.zeros((8, 128), np.float32)
    for e in range(8):
        e2[e, e * 16:(e + 1) * 16] = 2.0
    idf = np.eye(128, dtype=np.float32)

    nc = build_nc()
    in_maps = []
    for c in range(8):
        b, h = c // 2, c % 2
        ci = np.zeros((1, 4), np.int32)
        ci[0, 0] = ((2 * b) % 4) * 2048 * 1024
        in_maps.append({
            "x": np.ascontiguousarray(X[b, h * 1024:(h + 1) * 1024, :]),
            "wqkv": wqkv, "bqkv": bqkv, "wo": wo, "obr": obr,
            "wbt": wbt, "bbr": bbr, "wrt": wrt, "a2": a2, "b2": b2,
            "e2": e2, "idf": idf, "cinfo": ci,
        })
    res = run_bass_kernel_spmd(nc, in_maps, core_ids=list(range(8)))
    Y = np.empty((4, 2048, 4096), np.float32)
    for c in range(8):
        b, h = c // 2, c % 2
        Y[b, h * 1024:(h + 1) * 1024, :] = np.asarray(res.results[c]["y"])
    return Y


# version 08
# speedup vs baseline: 1.1052x; 1.1052x over previous
import contextlib
import numpy as np
import concourse.bass as bass
import concourse.mybir as mybir
from concourse.bass_utils import run_bass_kernel_spmd

F32 = mybir.dt.float32
BF16 = mybir.dt.bfloat16
I32 = mybir.dt.int32
AF = mybir.ActivationFunctionType
ALU = mybir.AluOpType

D = 4096
T = 1024
NH = 32
NCH = 4
EPS = 1e-5
SHIFT = -12.0
SCL = 1.0 / float(np.sqrt(128.0))

P_ORD = []
for ch in range(NCH):
    P_ORD += [16 + 4 * ch + k for k in range(4)] + [32 + 4 * ch + k for k in range(4)]
P_ORD += list(range(16))
N_PAN = 48
SETTOT = 16 * 7


def _qkv_dest(jcol):
    if jcol < 32:
        return ("q", jcol * 128)
    j2 = jcol - 32
    if j2 < 32:
        return ("kv", j2 // 8, (j2 % 8) * 128)
    j2 -= 32
    return ("kv", j2 // 8, 1024 + (j2 % 8) * 128)


LVL = 9


def build_nc(lvl=None):
    import os
    SUB = os.environ.get("KSUB", "hs")
    sub_ord = {"tp": 0, "e2": 1, "ce": 2, "hs": 3}[SUB]
    if lvl is None:
        lvl = LVL
    P1A, P1B, P1C, P2 = lvl >= 2, lvl >= 3, lvl >= 4, lvl >= 5
    P2R, P2T, P2H, P2Y = lvl >= 6, lvl >= 7, lvl >= 8, lvl >= 9
    nc = bass.Bass("TRN2", target_bir_lowering=False)

    x = nc.dram_tensor("x", [T, D], F32, kind="ExternalInput")
    wqkv = nc.dram_tensor("wqkv", [D, 3 * D], F32, kind="ExternalInput")
    bqkv = nc.dram_tensor("bqkv", [128, 96], F32, kind="ExternalInput")
    wo = nc.dram_tensor("wo", [D, D], F32, kind="ExternalInput")
    obr = nc.dram_tensor("obr", [1, D], F32, kind="ExternalInput")
    wbt = nc.dram_tensor("wbt", [D, D], F32, kind="ExternalInput")
    bbr = nc.dram_tensor("bbr", [1, D], F32, kind="ExternalInput")
    wrt = nc.dram_tensor("wrt", [128, 32, 8], F32, kind="ExternalInput")
    a2 = nc.dram_tensor("a2", [128, 32, 128], F32, kind="ExternalInput")
    b2 = nc.dram_tensor("b2", [128, D], F32, kind="ExternalInput")
    e2 = nc.dram_tensor("e2", [8, 128], F32, kind="ExternalInput")
    idf = nc.dram_tensor("idf", [128, 128], F32, kind="ExternalInput")
    cinfo = nc.dram_tensor("cinfo", [1, 4], I32, kind="ExternalInput")
    y = nc.dram_tensor("y", [T, D], F32, kind="ExternalOutput")

    qt_d = nc.dram_tensor("qt_d", [D, T], BF16)
    agin = nc.dram_tensor("agin", [NCH * 2048, T], BF16)
    agout = nc.dram_tensor("agout", [NCH * 8192, T], BF16)
    kvloc = nc.dram_tensor("kvloc", [NCH * 4096, T], BF16)
    x2_d = nc.dram_tensor("x2_d", [T, D], F32)

    es = contextlib.ExitStack()
    with es:
        sbt = lambda name, shape, dt: es.enter_context(nc.sbuf_tensor(name, shape, dt))
        AR1 = sbt("AR1", [128, 32768], BF16)
        AR2 = sbt("AR2", [128, 32768], BF16)
        U = sbt("U", [128, 4096], F32)
        xh_tm = sbt("xh_tm", [128, 4096], BF16)
        stage = sbt("stage", [128, 2, 1024], BF16)
        KT = sbt("KT", [128, 2, 2048], BF16)
        QT = sbt("QT", [128, 2, 1024], BF16)
        VA = sbt("VA", [128, 2, 16, 144], BF16)
        bqkv_sb = sbt("bqkv_sb", [128, 96], F32)
        obp = sbt("obp", [1, 2, 512], BF16)
        bbp = sbt("bbp", [1, 2, 512], BF16)
        wrt_sb = sbt("wrt_sb", [128, 32, 8], BF16)
        a2_sb = sbt("a2_sb", [128, 32, 128], BF16)
        b2_sb = sbt("b2_sb", [128, D], BF16)
        e2_sb = sbt("e2_sb", [8, 128], BF16)
        idf_sb = sbt("idf_sb", [128, 128], F32)
        ci_sb = sbt("ci_sb", [1, 4], I32)
        ones1 = sbt("ones1", [1, 128], BF16)
        cst = sbt("cst", [128, 2], F32)
        s_bnst = sbt("s_bnst", [128, 48], F32)
        s_bn = sbt("s_bn", [128, 2], F32)
        s_lnt = sbt("s_lnt", [128, 1], F32)
        s_r = sbt("s_r", [128, 1], F32)
        s_rec = sbt("s_rec", [128, 2], F32)
        s_l8 = sbt("s_l8", [128, 64], F32)
        s_d = sbt("s_d", [128, 8], F32)
        s_t = sbt("s_t", [128, 8], F32)
        s_sc1 = sbt("s_sc1", [128, 1], F32)
        s_w1 = sbt("s_w1", [128, 1], F32)
        s_w2 = sbt("s_w2", [128, 1], F32)
        s_eqA = sbt("s_eqA", [128, 8], F32)
        s_eqB = sbt("s_eqB", [128, 8], F32)
        s_comb = sbt("s_comb", [128, 64], F32)
        s_comb2 = sbt("s_comb2", [128, 64], F32)
        s_combT = sbt("s_combT", [8, 1024], BF16)
        s_lgs = sbt("s_lgs", [128, 64], F32)
        ce_sb = sbt("ce_sb", [128, 1024], BF16)
        hs_sb = sbt("hs_sb", [128, 1024], BF16)

        ps = es.enter_context(nc.psum_tensor("ps", [128, 4096], F32))

        AR3 = U.ap().rearrange("p (h n) -> p h n", h=2)
        def xblk(sl):
            return U[:, sl * 512:(sl + 1) * 512]
        def x2st(sl):
            return U[:, 1024 + sl * 512: 1024 + (sl + 1) * 512]
        def yst(sl):
            return U[:, 2048 + sl * 512: 2048 + (sl + 1) * 512]
        def ar1(a, b):
            return AR1[:, a:b]
        def ar2(a, b):
            return AR2[:, a:b]
        U_bf = U.ap().bitcast(BF16)
        def wslot(k, a, b):
            if k < 4:
                return AR2[:, k * 8192 + a: k * 8192 + b]
            return U_bf[:, a:b]
        def wslot3d(k):
            if k < 4:
                return AR2[:, k * 8192:(k + 1) * 8192].rearrange("p (cc n) -> p cc n", n=256)
            return U_bf.rearrange("p (cc n) -> p cc n", n=256)
        ar1_3d = AR1.ap().rearrange("p (cc n) -> p cc n", n=T)

        sems = {}
        def S(name):
            if name not in sems:
                sems[name] = es.enter_context(nc.semaphore(name))
            return sems[name]
        sCC = [S(f"CC{i}") for i in range(NCH)]
        rb = es.enter_context(nc.gpsimd.register("rb"))
        r2 = es.enter_context(nc.gpsimd.register("r2"))

        blk = es.enter_context(nc.Block())

        def barrier(eng, k):
            if lvl < k + 1:
                return
            eng.nop().then_inc(S("PH"), 1)
            eng.wait_ge(S("PH"), 5 * k)

        @blk.tensor
        def _(t):
            barrier(t, 1)
            if not P1A:
                return
            t.wait_ge(S("XT"), 16 * 8)
            for i in range(N_PAN):
                t.wait_ge(S(f"WP{i % 5}"), 16 * (i // 5 + 1))
                for jj in range(2):
                    tt = 2 * i + jj
                    if tt >= 2:
                        t.wait_ge(S("EV"), tt - 1)
                    pbase = 0 if tt % 2 == 0 else 1024
                    for half in range(2):
                        for cc in range(32):
                            ins = t.matmul(
                                ps[:, pbase + half * 512: pbase + (half + 1) * 512],
                                wslot(i % 5, cc * 256 + jj * 128, cc * 256 + (jj + 1) * 128),
                                ar1(cc * 1024 + half * 512, cc * 1024 + (half + 1) * 512),
                                start=(cc == 0), stop=(cc == 31))
                    ins.then_inc(S("QK"), 1)
            barrier(t, 2)
            if not P1B:
                return
            t.wait_ge(S("CST"), 5)

            def _scores(h):
                t.wait_ge(S(f"ATL{h % 2}"), 48 * (h // 2 + 1))
                for kt in range(16):
                    g = 16 * h + kt
                    if g >= 3:
                        t.wait_ge(S("EX"), g - 2)
                    base = (g % 3) * 1024
                    for half in range(2):
                        ins = t.matmul(
                            ps[:, base + half * 512: base + (half + 1) * 512],
                            KT[:, h % 2, kt * 128:(kt + 1) * 128],
                            QT[:, h % 2, half * 512:(half + 1) * 512],
                            start=True, stop=True)
                    ins.then_inc(S("SC"), 1)

            def _av(h):
                t.wait_ge(S("EX"), 16 * (h + 1))
                t.wait_ge(S(f"VTL{h % 2}"), 32 * (h // 2 + 1))
                for qs in range(8):
                    if 8 * h + qs >= 2:
                        t.wait_ge(S("CX"), 8 * h + qs - 1)
                    reg = ps[:, 3072 + (qs % 2) * 512: 3072 + (qs % 2) * 512 + 129]
                    for kt in range(16):
                        ins = t.matmul(
                            reg,
                            ar1((h % 2) * 16384 + kt * 1024 + qs * 128,
                                (h % 2) * 16384 + kt * 1024 + (qs + 1) * 128),
                            VA[:, h % 2, kt, 0:129],
                            start=(kt == 0), stop=(kt == 15))
                    ins.then_inc(S("AV"), 1)

            _scores(0)
            for h in range(NH):
                if h + 1 < NH:
                    _scores(h + 1)
                _av(h)
            barrier(t, 3)
            if not P1C:
                return
            t.wait_ge(S("TP"), 16 * 8)
            t.wait_ge(S("CST"), 3)
            for ob in range(8):
                t.wait_ge(S(f"WO{ob % 2}"), 32 * (ob // 2 + 1))
                for ns in range(8):
                    tt = 8 * ob + ns
                    if tt >= 4:
                        t.wait_ge(S("X2S"), tt - 3)
                    reg = ps[:, (tt % 4) * 512:(tt % 4 + 1) * 512]
                    for cc in range(32):
                        t.matmul(reg,
                                 ar1(cc * 1024 + ns * 128, cc * 1024 + (ns + 1) * 128),
                                 ar2((ob % 2) * 16384 + cc * 512, (ob % 2) * 16384 + (cc + 1) * 512),
                                 start=(cc == 0), stop=False)
                    ins = t.matmul(reg, ones1[0:1, 0:128],
                                   obp[0:1, ob % 2, :],
                                   start=False, stop=True)
                    ins.then_inc(S("OP"), 1)
            barrier(t, 4)
            if not P2R:
                return
            t.wait_ge(S("XT2"), 16 * 8)
            for ns in range(8):
                for cc in range(32):
                    ins = t.matmul(ps[:, 3072 + ns * 8: 3072 + (ns + 1) * 8],
                                   ar1(cc * 1024 + ns * 128, cc * 1024 + (ns + 1) * 128),
                                   wrt_sb[:, cc, :],
                                   start=(cc == 0), stop=(cc == 31))
            ins.then_inc(S("LG"), 1)
            if not P2T:
                return
            for cc in range(32):
                for half in range(2):
                    ins = t.matmul(ps[:, 1024 + half * 512: 1024 + (half + 1) * 512],
                                   a2_sb[:, cc, :],
                                   ar1(cc * 1024 + half * 512, cc * 1024 + (half + 1) * 512),
                                   start=(cc == 0), stop=(cc == 31))
            ins.then_inc(S("HM"), 1)
            if not P2H:
                return
            for ns in range(8):
                t.wait_ge(S("CMB"), ns + 1)
                t.transpose(ps[0:8, 2048 + ns * 128: 2048 + (ns + 1) * 128],
                            s_comb2[:, ns * 8:(ns + 1) * 8],
                            idf_sb[:]).then_inc(S("TT"), 1)
            if sub_ord < 1:
                return
            t.wait_ge(S("CT"), 1)
            for half in range(2):
                ins = t.matmul(ps[:, half * 512:(half + 1) * 512],
                               e2_sb[0:8, 0:128],
                               s_combT[0:8, half * 512:(half + 1) * 512],
                               start=True, stop=True)
            ins.then_inc(S("E2S"), 1)
            if not P2Y:
                return
            t.wait_ge(S("HS"), 1)
            for ob in range(8):
                t.wait_ge(S(f"WB{ob % 2}"), 32 * (ob // 2 + 1))
                for ns in range(8):
                    tt = 8 * ob + ns
                    if tt >= 4:
                        t.wait_ge(S("YS"), tt - 3)
                    reg = ps[:, (tt % 4) * 512:(tt % 4 + 1) * 512]
                    for cc in range(32):
                        t.matmul(reg,
                                 ar1(cc * 1024 + ns * 128, cc * 1024 + (ns + 1) * 128),
                                 ar2((ob % 2) * 16384 + cc * 512, (ob % 2) * 16384 + (cc + 1) * 512),
                                 start=(cc == 0), stop=False)
                    t.matmul(reg, hs_sb[:, ns * 128:(ns + 1) * 128],
                             b2_sb[:, ob * 512:(ob + 1) * 512],
                             start=False, stop=False)
                    ins = t.matmul(reg, ones1[0:1, 0:128],
                                   bbp[0:1, ob % 2, :],
                                   start=False, stop=True)
                    ins.then_inc(S("OP2"), 1)

        @blk.vector
        def _(v):
            _vc = [0]
            def vink(ins):
                _vc[0] += 1
                ins.then_inc(S("VCH"), 1)
            def vw():
                v.wait_ge(S("VCH"), _vc[0])
            v.memset(cst[:, 0:1], EPS).then_inc(S("CST"), 1)
            v.memset(cst[:, 1:2], SHIFT).then_inc(S("CST"), 1)
            v.memset(ones1[0:1, 0:128], 1.0).then_inc(S("CST"), 1)
            v.memset(VA[:, 0, :, 128:129], 1.0).then_inc(S("CST"), 1)
            v.memset(VA[:, 1, :, 128:129], 1.0).then_inc(S("CST"), 1)
            for ns in range(8):
                vw()
                for hlf in range(2):
                    v.wait_ge(S(f"X{hlf}"), 16 * (ns + 1))
                    for q in range(4):
                        vink(v.bn_stats(s_bnst[:, hlf * 24 + q * 6: hlf * 24 + (q + 1) * 6],
                                        AR3[:, hlf, q * 512:(q + 1) * 512]))
                vw()
                v.sem_inc(S("BN"), 1)
                if ns >= 1:
                    v.wait_ge(S("R"), ns)
                vw()
                vink(v.bn_aggr(s_bn[:], s_bnst[:, 0:48]))
                vw()
                v.sem_inc(S("BN"), 1)
                v.wait_ge(S("R"), ns + 1)
                if ns >= 1:
                    v.wait_ge(S("XT"), 16 * ns)
                vw()
                for hlf in range(2):
                    vink(v.tensor_scalar(xh_tm[:, hlf * 2048:(hlf + 1) * 2048],
                                         AR3[:, hlf, :], s_bn[:, 0:1], s_r[:],
                                         ALU.subtract, ALU.mult))
                    vw()
                    v.sem_inc(S("NM"), 1)
            barrier(v, 1)
            if not P1A:
                return
            v.wait_ge(S("SET"), SETTOT)
            for i in range(N_PAN):
                for jj in range(2):
                    tt = 2 * i + jj
                    jcol = P_ORD[i] * 2 + jj
                    v.wait_ge(S("QK"), tt + 1)
                    if tt >= 2:
                        v.wait_ge(S(f"STQ{tt % 2}"), 16 * (tt // 2))
                    pbase = 0 if tt % 2 == 0 else 1024
                    v.tensor_scalar(stage[:, tt % 2, :], ps[:, pbase:pbase + 1024],
                                    bqkv_sb[:, jcol:jcol + 1], None,
                                    ALU.add).then_inc(S("EV"), 1)
            barrier(v, 2)
            if not P1B:
                return
            for h in range(NH):
                for qs in range(8):
                    v.wait_ge(S("AV"), 8 * h + qs + 1)
                    reg = ps[:, 3072 + (qs % 2) * 512: 3072 + (qs % 2) * 512 + 129]
                    vw()
                    vink(v.reciprocal(s_rec[:, qs % 2:qs % 2 + 1], reg[:, 128:129]))
                    vw()
                    vink(v.tensor_scalar(ar2(qs * 4096 + h * 128, qs * 4096 + (h + 1) * 128),
                                         reg[:, 0:128], s_rec[:, qs % 2:qs % 2 + 1], None,
                                         ALU.mult))
                    vw()
                    v.sem_inc(S("CX"), 1)
            barrier(v, 3)
            if not P1C:
                return
            for ob in range(8):
                for ns in range(8):
                    tt = 8 * ob + ns
                    v.wait_ge(S("OP"), tt + 1)
                    v.wait_ge(S(f"XB{tt % 2}"), 16 * (tt // 2 + 1))
                    if tt >= 2:
                        v.wait_ge(S(f"X2W{tt % 2}"), 16 * (tt // 2))
                    v.tensor_tensor(x2st(tt % 2),
                                    ps[:, (tt % 4) * 512:(tt % 4 + 1) * 512],
                                    xblk(tt % 2), ALU.add).then_inc(S("X2S"), 1)
            barrier(v, 4)
            if not P2:
                return
            for ns in range(8):
                vw()
                for hlf in range(2):
                    v.wait_ge(S(f"X2L{hlf}"), 16 * (ns + 1))
                    for q in range(4):
                        vink(v.bn_stats(s_bnst[:, hlf * 24 + q * 6: hlf * 24 + (q + 1) * 6],
                                        AR3[:, hlf, q * 512:(q + 1) * 512]))
                vw()
                v.sem_inc(S("BN2"), 1)
                if ns >= 1:
                    v.wait_ge(S("R2"), ns)
                vw()
                vink(v.bn_aggr(s_bn[:], s_bnst[:, 0:48]))
                vw()
                v.sem_inc(S("BN2"), 1)
                v.wait_ge(S("R2"), ns + 1)
                if ns >= 1:
                    v.wait_ge(S("XT2"), 16 * ns)
                vw()
                for hlf in range(2):
                    vink(v.tensor_scalar(xh_tm[:, hlf * 2048:(hlf + 1) * 2048],
                                         AR3[:, hlf, :], s_bn[:, 0:1], s_r[:],
                                         ALU.subtract, ALU.mult))
                    vw()
                    v.sem_inc(S("NM2"), 1)
            if not P2R:
                return
            v.wait_ge(S("LG"), 1)
            vink(v.tensor_copy(s_lgs[:], ps[:, 3072:3136]))
            for ns in range(8):
                vw()
                vink(v.max(s_l8[:, ns * 8:(ns + 1) * 8],
                           s_lgs[:, ns * 8:(ns + 1) * 8]))
                vw()
                vink(v.tensor_tensor(s_d[:, ns:ns + 1], s_l8[:, ns * 8 + 1:ns * 8 + 2],
                                     s_l8[:, ns * 8:ns * 8 + 1], ALU.subtract))
                vw()
                v.sem_inc(S("D"), 1)
            for ns in range(8):
                v.wait_ge(S("T"), ns + 1)
                vw()
                vink(v.tensor_scalar(s_sc1[:], s_t[:, ns:ns + 1], 1.0, None, ALU.add))
                vw()
                vink(v.reciprocal(s_w1[:], s_sc1[:]))
                vw()
                vink(v.tensor_tensor(s_w2[:], s_t[:, ns:ns + 1], s_w1[:], ALU.mult))
                lg = s_lgs[:, ns * 8:(ns + 1) * 8]
                vink(v.tensor_scalar(s_eqA[:], lg, s_l8[:, ns * 8:ns * 8 + 1], None, ALU.is_equal))
                vw()
                vink(v.tensor_scalar(s_eqB[:], lg, s_l8[:, ns * 8 + 1:ns * 8 + 2], None, ALU.is_equal))
                vink(v.tensor_scalar(s_comb[:, ns * 8:(ns + 1) * 8], s_eqA[:], s_w1[:], None, ALU.mult))
                vw()
                vink(v.scalar_tensor_tensor(s_comb2[:, ns * 8:(ns + 1) * 8], s_eqB[:], s_w2[:],
                                            s_comb[:, ns * 8:(ns + 1) * 8],
                                            ALU.mult, ALU.add))
                vw()
                v.sem_inc(S("CMB"), 1)
            if not P2H:
                return
            v.wait_ge(S("TT"), 8)
            vink(v.tensor_copy(s_combT[0:8, :], ps[0:8, 2048:3072]))
            vw()
            v.sem_inc(S("CT"), 1)
            if sub_ord < 2:
                return
            v.wait_ge(S("E2S"), 1)
            vink(v.tensor_copy(ce_sb[:], ps[:, 0:1024]))
            vw()
            v.sem_inc(S("CE"), 1)
            if not P2H or sub_ord < 3:
                return
            v.wait_ge(S("HM"), 1)
            vw()
            vink(v.tensor_tensor(hs_sb[:], ps[:, 1024:2048], ce_sb[:], ALU.mult))
            vw()
            v.sem_inc(S("HS"), 1)
            if not P2Y:
                return
            for ob in range(8):
                for ns in range(8):
                    tt = 8 * ob + ns
                    v.wait_ge(S("OP2"), tt + 1)
                    v.wait_ge(S(f"XB2{tt % 2}"), 16 * (tt // 2 + 1))
                    if tt >= 2:
                        v.wait_ge(S(f"YW{tt % 2}"), 16 * (tt // 2))
                    v.tensor_tensor(yst(tt % 2),
                                    ps[:, (tt % 4) * 512:(tt % 4 + 1) * 512],
                                    xblk(tt % 2), ALU.add).then_inc(S("YS"), 1)

        @blk.scalar
        def _(sc):
            _ac = [0]
            sc.wait_ge(S("CST"), 1)
            for hlf in range(2):
                sc.dma_start(out=AR3[:, hlf, :],
                             in_=x[0:128, hlf * 2048:(hlf + 1) * 2048]
                             ).then_inc(S(f"X{hlf}"), 16)
            for ns in range(8):
                sc.wait_ge(S("BN"), 2 * ns + 2)
                if ns >= 1:
                    sc.wait_ge(S("NM"), 2 * ns)
                _ac[0] += 1
                sc.activation(s_lnt[:], s_bn[:, 1:2], AF.Ln, bias=cst[:, 0:1],
                              scale=1.0).then_inc(S("ACH"), 1)
                sc.wait_ge(S("ACH"), _ac[0])
                _ac[0] += 1
                sc.activation(s_r[:], s_lnt[:], AF.Exp, scale=-0.5).then_inc(S("ACH"), 1)
                sc.wait_ge(S("ACH"), _ac[0])
                sc.sem_inc(S("R"), 1)
                if ns + 1 < 8:
                    for hlf in range(2):
                        sc.wait_ge(S("NM"), 2 * ns + hlf + 1)
                        sc.dma_start(out=AR3[:, hlf, :],
                                     in_=x[(ns + 1) * 128:(ns + 2) * 128,
                                           hlf * 2048:(hlf + 1) * 2048]
                                     ).then_inc(S(f"X{hlf}"), 16)
            barrier(sc, 1)
            barrier(sc, 2)
            if not P1B:
                return
            sc.wait_ge(S("CST"), 2)
            for h in range(NH):
                for kt in range(16):
                    g = 16 * h + kt
                    sc.wait_ge(S("SC"), g + 1)
                    if kt == 0 and h >= 2:
                        sc.wait_ge(S("AV"), 8 * (h - 1))
                    base = (g % 3) * 1024
                    sc.activation(ar1((h % 2) * 16384 + kt * 1024, (h % 2) * 16384 + (kt + 1) * 1024),
                                  ps[:, base:base + 1024], AF.Exp,
                                  bias=cst[:, 1:2], scale=SCL).then_inc(S("EX"), 1)
            barrier(sc, 3)
            if not P1C:
                return
            def _xsrc(tt):
                ob_, ns_ = tt // 8, tt % 8
                return x[ns_ * 128:(ns_ + 1) * 128, ob_ * 512:(ob_ + 1) * 512]
            sc.wait_ge(S("QK"), 96)
            for tt in range(2):
                sc.dma_start(out=xblk(tt % 2), in_=_xsrc(tt)).then_inc(S(f"XB{tt % 2}"), 16)
            for tt in range(64):
                ob_, ns_ = tt // 8, tt % 8
                sc.wait_ge(S("X2S"), tt + 1)
                sc.dma_start(out=x2_d[ns_ * 128:(ns_ + 1) * 128, ob_ * 512:(ob_ + 1) * 512],
                             in_=x2st(tt % 2)).then_inc(S(f"X2W{tt % 2}"), 16)
                if tt + 2 < 64:
                    sc.dma_start(out=xblk(tt % 2), in_=_xsrc(tt + 2)).then_inc(S(f"XB{tt % 2}"), 16)
            barrier(sc, 4)
            if not P2:
                return
            sc.wait_ge(S("X2W0"), 16 * 32)
            sc.wait_ge(S("X2W1"), 16 * 32)
            for hlf in range(2):
                sc.dma_start(out=AR3[:, hlf, :],
                             in_=x2_d[0:128, hlf * 2048:(hlf + 1) * 2048]
                             ).then_inc(S(f"X2L{hlf}"), 16)
            for ns in range(8):
                sc.wait_ge(S("BN2"), 2 * ns + 2)
                if ns >= 1:
                    sc.wait_ge(S("NM2"), 2 * ns)
                _ac[0] += 1
                sc.activation(s_lnt[:], s_bn[:, 1:2], AF.Ln, bias=cst[:, 0:1],
                              scale=1.0).then_inc(S("ACH"), 1)
                sc.wait_ge(S("ACH"), _ac[0])
                _ac[0] += 1
                sc.activation(s_r[:], s_lnt[:], AF.Exp, scale=-0.5).then_inc(S("ACH"), 1)
                sc.wait_ge(S("ACH"), _ac[0])
                sc.sem_inc(S("R2"), 1)
                if ns + 1 < 8:
                    for hlf in range(2):
                        sc.wait_ge(S("NM2"), 2 * ns + hlf + 1)
                        sc.dma_start(out=AR3[:, hlf, :],
                                     in_=x2_d[(ns + 1) * 128:(ns + 2) * 128,
                                              hlf * 2048:(hlf + 1) * 2048]
                                     ).then_inc(S(f"X2L{hlf}"), 16)
            if not P2R:
                return
            for ns in range(8):
                sc.wait_ge(S("D"), ns + 1)
                sc.activation(s_t[:, ns:ns + 1], s_d[:, ns:ns + 1], AF.Exp,
                              scale=1.0).then_inc(S("T"), 1)
            if not P2Y:
                return
            sc.wait_ge(S("NM2"), 16)
            def _x2src(tt):
                ob_, ns_ = tt // 8, tt % 8
                return x2_d[ns_ * 128:(ns_ + 1) * 128, ob_ * 512:(ob_ + 1) * 512]
            for tt in range(2):
                sc.dma_start(out=xblk(tt % 2), in_=_x2src(tt)).then_inc(S(f"XB2{tt % 2}"), 16)
            for tt in range(64):
                ob_, ns_ = tt // 8, tt % 8
                sc.wait_ge(S("YS"), tt + 1)
                sc.dma_start(out=y[ns_ * 128:(ns_ + 1) * 128, ob_ * 512:(ob_ + 1) * 512],
                             in_=yst(tt % 2)).then_inc(S(f"YW{tt % 2}"), 16)
                if tt + 2 < 64:
                    sc.dma_start(out=xblk(tt % 2), in_=_x2src(tt + 2)).then_inc(S(f"XB2{tt % 2}"), 16)
            sc.wait_ge(S("YW0"), 16 * 32)
            sc.wait_ge(S("YW1"), 16 * 32)

        @blk.sync
        def _(s):
            for ns in range(8):
                s.wait_ge(S("NM"), 2 * (ns + 1))
                s.dma_start(out=ar1_3d[:, :, ns * 128:(ns + 1) * 128],
                            in_=xh_tm[:], transpose=True).then_inc(S("XT"), 16)
            barrier(s, 1)
            if not P1A:
                return
            for i in range(N_PAN):
                for jj in range(2):
                    tt = 2 * i + jj
                    jcol = P_ORD[i] * 2 + jj
                    s.wait_ge(S("EV"), tt + 1)
                    dest = _qkv_dest(jcol)
                    if dest[0] == "q":
                        dst = qt_d[dest[1]:dest[1] + 128, :]
                    else:
                        dst = agin[dest[1] * 2048 + dest[2]: dest[1] * 2048 + dest[2] + 128, :]
                    s.dma_start(out=dst, in_=stage[:, tt % 2, :]).then_inc(S(f"STQ{tt % 2}"), 16)
            barrier(s, 2)
            if not P1B:
                return
            s.wait_ge(S("STQ0"), 16 * 48)
            s.wait_ge(S("STQ1"), 16 * 48)
            for h in range(NH):
                ch, hl = h // 8, h % 8
                s.wait_ge(S(f"KVL{ch}"), 16)
                if h >= 2:
                    s.wait_ge(S("SC"), 16 * (h - 1))
                    s.wait_ge(S("AV"), 8 * (h - 1))
                s.dma_start(out=KT[:, h % 2, 0:1024],
                            in_=kvloc[ch * 4096 + hl * 128: ch * 4096 + (hl + 1) * 128, :]).then_inc(S(f"ATL{h % 2}"), 16)
                s.dma_start(out=KT[:, h % 2, 1024:2048],
                            in_=kvloc[ch * 4096 + 2048 + hl * 128: ch * 4096 + 2048 + (hl + 1) * 128, :]).then_inc(S(f"ATL{h % 2}"), 16)
                s.dma_start(out=QT[:, h % 2, :],
                            in_=qt_d[h * 128:(h + 1) * 128, :]).then_inc(S(f"ATL{h % 2}"), 16)
                s.dma_start(out=VA[:, h % 2, 0:8, 0:128],
                            in_=kvloc[ch * 4096 + 1024 + hl * 128: ch * 4096 + 1024 + (hl + 1) * 128, :],
                            transpose=True).then_inc(S(f"VTL{h % 2}"), 16)
                s.dma_start(out=VA[:, h % 2, 8:16, 0:128],
                            in_=kvloc[ch * 4096 + 3072 + hl * 128: ch * 4096 + 3072 + (hl + 1) * 128, :],
                            transpose=True).then_inc(S(f"VTL{h % 2}"), 16)
            barrier(s, 3)
            if not P1C:
                return
            s.wait_ge(S("CX"), 256)
            for qs in range(8):
                s.dma_start(out=ar1_3d[:, :, qs * 128:(qs + 1) * 128],
                            in_=AR2[:, qs * 4096:(qs + 1) * 4096], transpose=True
                            ).then_inc(S("TP"), 16)
            barrier(s, 4)
            if not P2:
                return
            s.wait_ge(S("OP"), 64)
            for ns in range(8):
                s.wait_ge(S("NM2"), 2 * (ns + 1))
                s.dma_start(out=ar1_3d[:, :, ns * 128:(ns + 1) * 128],
                            in_=xh_tm[:], transpose=True).then_inc(S("XT2"), 16)

        @blk.gpsimd
        def _(g):
            g.dma_start(out=ci_sb[:], in_=cinfo.ap()).then_inc(S("SET"), 16)
            g.dma_start(out=bqkv_sb[:], in_=bqkv.ap()).then_inc(S("SET"), 16)
            g.dma_start(out=wrt_sb[:], in_=wrt.ap()).then_inc(S("SET"), 16)
            g.dma_start(out=a2_sb[:], in_=a2.ap()).then_inc(S("SET"), 16)
            g.dma_start(out=b2_sb[:], in_=b2.ap()).then_inc(S("SET"), 16)
            g.dma_start(out=e2_sb[:], in_=e2.ap()).then_inc(S("SET"), 16)
            g.dma_start(out=idf_sb[:], in_=idf.ap()).then_inc(S("SET"), 16)
            g.wait_ge(S("SET"), SETTOT)
            g.reg_load(rb, ci_sb[0:1, 0:1])
            barrier(g, 1)
            if not P1A:
                return
            wq_r = wqkv.ap().rearrange("(cc p) c -> p cc c", p=128)
            for i in range(N_PAN):
                p = P_ORD[i]
                if i == 4:
                    g.wait_ge(S("NM"), 16)
                if i >= 5:
                    g.wait_ge(S("QK"), 2 * (i - 4))
                g.dma_start(out=wslot3d(i % 5),
                            in_=wq_r[:, :, p * 256:(p + 1) * 256]).then_inc(S(f"WP{i % 5}"), 16)
                if i >= 12 and (i - 12) % 8 == 0 and (i - 12) // 8 < NCH:
                    ch = (i - 12) // 8
                    g.wait_ge(S("STQ0"), 16 * (8 * ch + 8))
                    g.wait_ge(S("STQ1"), 16 * (8 * ch + 8))
                    g.collective_compute(
                        "AllGather", ALU.bypass,
                        replica_groups=[[0, 1, 2, 3], [4, 5, 6, 7]],
                        ins=[agin[ch * 2048:(ch + 1) * 2048, :].opt()],
                        outs=[agout[ch * 8192:(ch + 1) * 8192, :].opt()],
                    ).then_inc(sCC[ch], 1)
            for ch in range(NCH):
                g.wait_ge(sCC[ch], 1)
                g.reg_mov(r2, ch * 8192 * 1024)
                g.reg_add(r2, r2, rb)
                g.dma_start(out=kvloc[ch * 4096:(ch + 1) * 4096, :],
                            in_=bass.AP(agout, r2, [[1024, 4096], [1, 1024]])
                            ).then_inc(S(f"KVL{ch}"), 16)
            barrier(g, 2)
            barrier(g, 3)
            if not P1C:
                return
            wo_r = wo.ap().rearrange("(cc p) o -> p cc o", p=128)
            g.wait_ge(S("TP"), 16 * 8)
            for ob in range(8):
                if ob >= 2:
                    g.wait_ge(S("OP"), 8 * (ob - 1))
                g.dma_start(out=AR2[:, (ob % 2) * 16384:(ob % 2 + 1) * 16384]
                            .rearrange("p (cc n) -> p cc n", n=512),
                            in_=wo_r[:, :, ob * 512:(ob + 1) * 512]).then_inc(S(f"WO{ob % 2}"), 16)
                g.dma_start(out=obp[0:1, ob % 2, :],
                            in_=obr.ap()[0:1, ob * 512:(ob + 1) * 512]).then_inc(S(f"WO{ob % 2}"), 16)
            barrier(g, 4)
            if not P2Y:
                return
            wb_r = wbt.ap().rearrange("(cc p) o -> p cc o", p=128)
            g.wait_ge(S("OP"), 64)
            for ob in range(8):
                if ob >= 2:
                    g.wait_ge(S("OP2"), 8 * (ob - 1))
                g.dma_start(out=AR2[:, (ob % 2) * 16384:(ob % 2 + 1) * 16384]
                            .rearrange("p (cc n) -> p cc n", n=512),
                            in_=wb_r[:, :, ob * 512:(ob + 1) * 512]).then_inc(S(f"WB{ob % 2}"), 16)
                g.dma_start(out=bbp[0:1, ob % 2, :],
                            in_=bbr.ap()[0:1, ob * 512:(ob + 1) * 512]).then_inc(S(f"WB{ob % 2}"), 16)

    return nc


def kernel(**inputs):
    X = np.ascontiguousarray(np.asarray(inputs["L0_attn_x"], dtype=np.float32))
    wqkv = np.ascontiguousarray(np.asarray(inputs["L0_attn_attn_qkvw"], dtype=np.float32))
    qkvb = np.asarray(inputs["L0_attn_attn_qkvb"], dtype=np.float32)
    wo = np.ascontiguousarray(np.asarray(inputs["L0_attn_attn_ow"], dtype=np.float32))
    ob_ = np.asarray(inputs["L0_attn_attn_ob"], dtype=np.float32)
    wb = np.asarray(inputs["L0_ffn_W_base"], dtype=np.float32)
    bb = np.asarray(inputs["L0_ffn_b_base"], dtype=np.float32)
    wr = np.asarray(inputs["L0_ffn_W_router"], dtype=np.float32)
    A = np.asarray(inputs["L0_ffn_A_stack"], dtype=np.float32)
    B = np.asarray(inputs["L0_ffn_B_stack"], dtype=np.float32)

    bqkv = np.ascontiguousarray(qkvb.reshape(96, 128).T)
    obr = np.ascontiguousarray(ob_.reshape(1, 4096))
    wbt = np.ascontiguousarray(wb.T)
    bbr = np.ascontiguousarray(bb.reshape(1, 4096))
    wrt = np.ascontiguousarray(wr.T.reshape(32, 128, 8).transpose(1, 0, 2))
    a2 = np.ascontiguousarray(A.transpose(2, 0, 1).reshape(32, 128, 128).transpose(1, 0, 2))
    b2 = np.ascontiguousarray(B.transpose(0, 2, 1).reshape(128, 4096))
    e2 = np.zeros((8, 128), np.float32)
    for e in range(8):
        e2[e, e * 16:(e + 1) * 16] = 2.0
    idf = np.eye(128, dtype=np.float32)

    nc = build_nc()
    in_maps = []
    for c in range(8):
        b, h = c // 2, c % 2
        ci = np.zeros((1, 4), np.int32)
        ci[0, 0] = ((2 * b) % 4) * 2048 * 1024
        in_maps.append({
            "x": np.ascontiguousarray(X[b, h * 1024:(h + 1) * 1024, :]),
            "wqkv": wqkv, "bqkv": bqkv, "wo": wo, "obr": obr,
            "wbt": wbt, "bbr": bbr, "wrt": wrt, "a2": a2, "b2": b2,
            "e2": e2, "idf": idf, "cinfo": ci,
        })
    res = run_bass_kernel_spmd(nc, in_maps, core_ids=list(range(8)))
    Y = np.empty((4, 2048, 4096), np.float32)
    for c in range(8):
        b, h = c // 2, c % 2
        Y[b, h * 1024:(h + 1) * 1024, :] = np.asarray(res.results[c]["y"])
    return Y


# version 09
# speedup vs baseline: 1.2172x; 1.1013x over previous
import contextlib
import numpy as np
import concourse.bass as bass
import concourse.mybir as mybir
from concourse.bass_utils import run_bass_kernel_spmd

F32 = mybir.dt.float32
BF16 = mybir.dt.bfloat16
I32 = mybir.dt.int32
AF = mybir.ActivationFunctionType
ALU = mybir.AluOpType

D = 4096
T = 1024
NH = 32
NCH = 4
EPS = 1e-5
SHIFT = -12.0
SCL = 1.0 / float(np.sqrt(128.0))

P_ORD = []
for ch in range(NCH):
    P_ORD += [16 + 4 * ch + k for k in range(4)] + [32 + 4 * ch + k for k in range(4)]
P_ORD += list(range(16))
N_PAN = 48
SETTOT = 16 * 7


def _qkv_dest(jcol):
    if jcol < 32:
        return ("q", jcol * 128)
    j2 = jcol - 32
    if j2 < 32:
        return ("kv", j2 // 8, (j2 % 8) * 128)
    j2 -= 32
    return ("kv", j2 // 8, 1024 + (j2 % 8) * 128)


LVL = 9


def build_nc(lvl=None):
    import os
    SUB = os.environ.get("KSUB", "hs")
    sub_ord = {"tp": 0, "e2": 1, "ce": 2, "hs": 3}[SUB]
    if lvl is None:
        lvl = LVL
    P1A, P1B, P1C, P2 = lvl >= 2, lvl >= 3, lvl >= 4, lvl >= 5
    P2R, P2T, P2H, P2Y = lvl >= 6, lvl >= 7, lvl >= 8, lvl >= 9
    nc = bass.Bass("TRN2", target_bir_lowering=False)

    x = nc.dram_tensor("x", [T, D], F32, kind="ExternalInput")
    wqkv = nc.dram_tensor("wqkv", [D, 3 * D], F32, kind="ExternalInput")
    bqkv = nc.dram_tensor("bqkv", [128, 96], F32, kind="ExternalInput")
    wo = nc.dram_tensor("wo", [D, D], F32, kind="ExternalInput")
    obr = nc.dram_tensor("obr", [1, D], F32, kind="ExternalInput")
    wbt = nc.dram_tensor("wbt", [D, D], F32, kind="ExternalInput")
    bbr = nc.dram_tensor("bbr", [1, D], F32, kind="ExternalInput")
    wrt = nc.dram_tensor("wrt", [128, 32, 8], F32, kind="ExternalInput")
    a2 = nc.dram_tensor("a2", [128, 32, 128], F32, kind="ExternalInput")
    b2 = nc.dram_tensor("b2", [128, D], F32, kind="ExternalInput")
    e2 = nc.dram_tensor("e2", [8, 128], F32, kind="ExternalInput")
    idf = nc.dram_tensor("idf", [128, 128], F32, kind="ExternalInput")
    cinfo = nc.dram_tensor("cinfo", [1, 4], I32, kind="ExternalInput")
    y = nc.dram_tensor("y", [T, D], F32, kind="ExternalOutput")

    qt_d = nc.dram_tensor("qt_d", [D, T], BF16)
    agin = nc.dram_tensor("agin", [NCH * 2048, T], BF16)
    agout = nc.dram_tensor("agout", [NCH * 8192, T], BF16)
    kvloc = nc.dram_tensor("kvloc", [NCH * 4096, T], BF16)
    x2_d = nc.dram_tensor("x2_d", [T, D], F32)

    es = contextlib.ExitStack()
    with es:
        sbt = lambda name, shape, dt: es.enter_context(nc.sbuf_tensor(name, shape, dt))
        AR1 = sbt("AR1", [128, 32768], BF16)
        AR2 = sbt("AR2", [128, 32768], BF16)
        U = sbt("U", [128, 4096], F32)
        xh_tm = sbt("xh_tm", [128, 4096], BF16)
        stage = sbt("stage", [128, 2, 1024], BF16)
        KT = sbt("KT", [128, 2, 2048], BF16)
        QT = sbt("QT", [128, 2, 1024], BF16)
        VA = sbt("VA", [128, 2, 16, 144], BF16)
        bqkv_sb = sbt("bqkv_sb", [128, 96], F32)
        obp = sbt("obp", [1, 2, 512], BF16)
        bbp = sbt("bbp", [1, 2, 512], BF16)
        wrt_sb = sbt("wrt_sb", [128, 32, 8], BF16)
        a2_sb = sbt("a2_sb", [128, 32, 128], BF16)
        b2_sb = sbt("b2_sb", [128, D], BF16)
        e2_sb = sbt("e2_sb", [8, 128], BF16)
        idf_sb = sbt("idf_sb", [128, 128], F32)
        ci_sb = sbt("ci_sb", [1, 4], I32)
        ones1 = sbt("ones1", [1, 128], BF16)
        cst = sbt("cst", [128, 2], F32)
        s_bnst = sbt("s_bnst", [128, 48], F32)
        s_bn = sbt("s_bn", [128, 2], F32)
        s_lnt = sbt("s_lnt", [128, 1], F32)
        s_r = sbt("s_r", [128, 1], F32)
        s_rec = sbt("s_rec", [128, 2], F32)
        s_l8 = sbt("s_l8", [128, 64], F32)
        s_d = sbt("s_d", [128, 8], F32)
        s_t = sbt("s_t", [128, 8], F32)
        s_sc1 = sbt("s_sc1", [128, 1], F32)
        s_w1 = sbt("s_w1", [128, 1], F32)
        s_w2 = sbt("s_w2", [128, 1], F32)
        s_eqA = sbt("s_eqA", [128, 8], F32)
        s_eqB = sbt("s_eqB", [128, 8], F32)
        s_comb = sbt("s_comb", [128, 64], F32)
        s_comb2 = sbt("s_comb2", [128, 64], F32)
        s_combT = sbt("s_combT", [8, 1024], BF16)
        s_lgs = sbt("s_lgs", [128, 64], F32)
        ce_sb = sbt("ce_sb", [128, 1024], BF16)
        hs_sb = sbt("hs_sb", [128, 1024], BF16)

        ps = es.enter_context(nc.psum_tensor("ps", [128, 4096], F32))

        AR3 = U.ap().rearrange("p (h n) -> p h n", h=2)
        def xblk(sl):
            return U[:, sl * 512:(sl + 1) * 512]
        def x2st(sl):
            return U[:, 1024 + sl * 512: 1024 + (sl + 1) * 512]
        def yst(sl):
            return U[:, 2048 + sl * 512: 2048 + (sl + 1) * 512]
        def ar1(a, b):
            return AR1[:, a:b]
        def ar2(a, b):
            return AR2[:, a:b]
        U_bf = U.ap().bitcast(BF16)
        AR2_f32 = AR2.ap().bitcast(F32)
        KT_flat = KT.ap().rearrange("p a b -> p (a b)")
        def xtile(par, hlf, a, b):
            if par == 0:
                return U[:, hlf * 2048 + a: hlf * 2048 + b]
            return AR2_f32[:, hlf * 2048 + a: hlf * 2048 + b]
        def xhbuf(par):
            return xh_tm if par == 0 else KT_flat
        def wslot(k, a, b):
            if k < 4:
                return AR2[:, k * 8192 + a: k * 8192 + b]
            return U_bf[:, a:b]
        def wslot3d(k):
            if k < 4:
                return AR2[:, k * 8192:(k + 1) * 8192].rearrange("p (cc n) -> p cc n", n=256)
            return U_bf.rearrange("p (cc n) -> p cc n", n=256)
        ar1_3d = AR1.ap().rearrange("p (cc n) -> p cc n", n=T)

        sems = {}
        def S(name):
            if name not in sems:
                sems[name] = es.enter_context(nc.semaphore(name))
            return sems[name]
        sCC = [S(f"CC{i}") for i in range(NCH)]
        rb = es.enter_context(nc.gpsimd.register("rb"))
        r2 = es.enter_context(nc.gpsimd.register("r2"))

        blk = es.enter_context(nc.Block())

        def barrier(eng, k):
            if lvl < k + 1:
                return
            eng.nop().then_inc(S("PH"), 1)
            eng.wait_ge(S("PH"), 5 * k)

        @blk.tensor
        def _(t):
            barrier(t, 1)
            if not P1A:
                return
            t.wait_ge(S("XT0"), 16 * 4)
            t.wait_ge(S("XT1"), 16 * 4)
            for i in range(N_PAN):
                t.wait_ge(S(f"WP{i % 5}"), 16 * (i // 5 + 1))
                for jj in range(2):
                    tt = 2 * i + jj
                    if tt >= 2:
                        t.wait_ge(S("EV"), tt - 1)
                    pbase = 0 if tt % 2 == 0 else 1024
                    for half in range(2):
                        for cc in range(32):
                            ins = t.matmul(
                                ps[:, pbase + half * 512: pbase + (half + 1) * 512],
                                wslot(i % 5, cc * 256 + jj * 128, cc * 256 + (jj + 1) * 128),
                                ar1(cc * 1024 + half * 512, cc * 1024 + (half + 1) * 512),
                                start=(cc == 0), stop=(cc == 31))
                    ins.then_inc(S("QK"), 1)
            barrier(t, 2)
            if not P1B:
                return
            t.wait_ge(S("CST"), 5)

            def _score_kt(h, kt):
                g = 16 * h + kt
                if g >= 3:
                    t.wait_ge(S("EX"), g - 2)
                base = (g % 3) * 1024
                for half in range(2):
                    ins = t.matmul(
                        ps[:, base + half * 512: base + (half + 1) * 512],
                        KT[:, h % 2, kt * 128:(kt + 1) * 128],
                        QT[:, h % 2, half * 512:(half + 1) * 512],
                        start=True, stop=True)
                ins.then_inc(S("SC"), 1)

            def _av_qs(h, qs):
                if qs == 0:
                    t.wait_ge(S("EX"), 16 * (h + 1))
                    t.wait_ge(S(f"VTL{h % 2}"), 32 * (h // 2 + 1))
                if 8 * h + qs >= 2:
                    t.wait_ge(S("CX"), 8 * h + qs - 1)
                reg = ps[:, 3072 + (qs % 2) * 512: 3072 + (qs % 2) * 512 + 129]
                for kt in range(16):
                    ins = t.matmul(
                        reg,
                        ar1((h % 2) * 16384 + kt * 1024 + qs * 128,
                            (h % 2) * 16384 + kt * 1024 + (qs + 1) * 128),
                        VA[:, h % 2, kt, 0:129],
                        start=(kt == 0), stop=(kt == 15))
                ins.then_inc(S("AV"), 1)

            t.wait_ge(S("ATL0"), 48)
            for kt in range(16):
                _score_kt(0, kt)
            for h in range(NH):
                if h + 1 < NH:
                    t.wait_ge(S(f"ATL{(h + 1) % 2}"), 48 * ((h + 1) // 2 + 1))
                    for j in range(8):
                        _score_kt(h + 1, 2 * j)
                        _score_kt(h + 1, 2 * j + 1)
                        _av_qs(h, j)
                else:
                    for j in range(8):
                        _av_qs(h, j)
            barrier(t, 3)
            if not P1C:
                return
            t.wait_ge(S("TP"), 16 * 8)
            t.wait_ge(S("CST"), 3)
            for ob in range(8):
                t.wait_ge(S(f"WO{ob % 2}"), 32 * (ob // 2 + 1))
                for ns in range(8):
                    tt = 8 * ob + ns
                    if tt >= 4:
                        t.wait_ge(S("X2S"), tt - 3)
                    reg = ps[:, (tt % 4) * 512:(tt % 4 + 1) * 512]
                    for cc in range(32):
                        t.matmul(reg,
                                 ar1(cc * 1024 + ns * 128, cc * 1024 + (ns + 1) * 128),
                                 ar2((ob % 2) * 16384 + cc * 512, (ob % 2) * 16384 + (cc + 1) * 512),
                                 start=(cc == 0), stop=False)
                    ins = t.matmul(reg, ones1[0:1, 0:128],
                                   obp[0:1, ob % 2, :],
                                   start=False, stop=True)
                    ins.then_inc(S("OP"), 1)
            barrier(t, 4)
            if not P2R:
                return
            t.wait_ge(S("XT20"), 16 * 4)
            t.wait_ge(S("XT21"), 16 * 4)
            for ns in range(8):
                for cc in range(32):
                    ins = t.matmul(ps[:, 3072 + ns * 8: 3072 + (ns + 1) * 8],
                                   ar1(cc * 1024 + ns * 128, cc * 1024 + (ns + 1) * 128),
                                   wrt_sb[:, cc, :],
                                   start=(cc == 0), stop=(cc == 31))
            ins.then_inc(S("LG"), 1)
            if not P2T:
                return
            for cc in range(32):
                for half in range(2):
                    ins = t.matmul(ps[:, 1024 + half * 512: 1024 + (half + 1) * 512],
                                   a2_sb[:, cc, :],
                                   ar1(cc * 1024 + half * 512, cc * 1024 + (half + 1) * 512),
                                   start=(cc == 0), stop=(cc == 31))
            ins.then_inc(S("HM"), 1)
            if not P2H:
                return
            for ns in range(8):
                t.wait_ge(S("CMB"), ns + 1)
                t.transpose(ps[0:8, 2048 + ns * 128: 2048 + (ns + 1) * 128],
                            s_comb2[:, ns * 8:(ns + 1) * 8],
                            idf_sb[:]).then_inc(S("TT"), 1)
            if sub_ord < 1:
                return
            t.wait_ge(S("CT"), 1)
            for half in range(2):
                ins = t.matmul(ps[:, half * 512:(half + 1) * 512],
                               e2_sb[0:8, 0:128],
                               s_combT[0:8, half * 512:(half + 1) * 512],
                               start=True, stop=True)
            ins.then_inc(S("E2S"), 1)
            if not P2Y:
                return
            t.wait_ge(S("HS"), 1)
            for ob in range(8):
                t.wait_ge(S(f"WB{ob % 2}"), 32 * (ob // 2 + 1))
                for ns in range(8):
                    tt = 8 * ob + ns
                    if tt >= 4:
                        t.wait_ge(S("YS"), tt - 3)
                    reg = ps[:, (tt % 4) * 512:(tt % 4 + 1) * 512]
                    for cc in range(32):
                        t.matmul(reg,
                                 ar1(cc * 1024 + ns * 128, cc * 1024 + (ns + 1) * 128),
                                 ar2((ob % 2) * 16384 + cc * 512, (ob % 2) * 16384 + (cc + 1) * 512),
                                 start=(cc == 0), stop=False)
                    t.matmul(reg, hs_sb[:, ns * 128:(ns + 1) * 128],
                             b2_sb[:, ob * 512:(ob + 1) * 512],
                             start=False, stop=False)
                    ins = t.matmul(reg, ones1[0:1, 0:128],
                                   bbp[0:1, ob % 2, :],
                                   start=False, stop=True)
                    ins.then_inc(S("OP2"), 1)

        @blk.vector
        def _(v):
            _vc = [0]
            def vink(ins):
                _vc[0] += 1
                ins.then_inc(S("VCH"), 1)
            def vw():
                v.wait_ge(S("VCH"), _vc[0])
            v.memset(cst[:, 0:1], EPS).then_inc(S("CST"), 1)
            v.memset(cst[:, 1:2], SHIFT).then_inc(S("CST"), 1)
            v.memset(ones1[0:1, 0:128], 1.0).then_inc(S("CST"), 1)
            v.memset(VA[:, 0, :, 128:129], 1.0).then_inc(S("CST"), 1)
            v.memset(VA[:, 1, :, 128:129], 1.0).then_inc(S("CST"), 1)
            for ns in range(8):
                vw()
                for hlf in range(2):
                    v.wait_ge(S(f"X{(ns % 2) * 2 + hlf}"), 16 * (ns // 2 + 1))
                    for q in range(4):
                        vink(v.bn_stats(s_bnst[:, hlf * 24 + q * 6: hlf * 24 + (q + 1) * 6],
                                        xtile(ns % 2, hlf, q * 512, (q + 1) * 512)))
                vw()
                v.sem_inc(S("BN"), 1)
                if ns >= 1:
                    v.wait_ge(S("R"), ns)
                vw()
                vink(v.bn_aggr(s_bn[:], s_bnst[:, 0:48]))
                vw()
                v.sem_inc(S("BN"), 1)
                v.wait_ge(S("R"), ns + 1)
                if ns >= 2:
                    v.wait_ge(S(f"XT{ns % 2}"), 16 * (ns // 2))
                vw()
                for hlf in range(2):
                    vink(v.tensor_scalar(xhbuf(ns % 2)[:, hlf * 2048:(hlf + 1) * 2048],
                                         xtile(ns % 2, hlf, 0, 2048), s_bn[:, 0:1], s_r[:],
                                         ALU.subtract, ALU.mult))
                    vw()
                    v.sem_inc(S("NM"), 1)
            barrier(v, 1)
            if not P1A:
                return
            v.wait_ge(S("SET"), SETTOT)
            for i in range(N_PAN):
                for jj in range(2):
                    tt = 2 * i + jj
                    jcol = P_ORD[i] * 2 + jj
                    v.wait_ge(S("QK"), tt + 1)
                    if tt >= 2:
                        v.wait_ge(S(f"STQ{tt % 2}"), 16 * (tt // 2))
                    pbase = 0 if tt % 2 == 0 else 1024
                    v.tensor_scalar(stage[:, tt % 2, :], ps[:, pbase:pbase + 1024],
                                    bqkv_sb[:, jcol:jcol + 1], None,
                                    ALU.add).then_inc(S("EV"), 1)
            barrier(v, 2)
            if not P1B:
                return
            for h in range(NH):
                for qs in range(8):
                    v.wait_ge(S("AV"), 8 * h + qs + 1)
                    reg = ps[:, 3072 + (qs % 2) * 512: 3072 + (qs % 2) * 512 + 129]
                    vw()
                    vink(v.reciprocal(s_rec[:, qs % 2:qs % 2 + 1], reg[:, 128:129]))
                    vw()
                    vink(v.tensor_scalar(ar2(qs * 4096 + h * 128, qs * 4096 + (h + 1) * 128),
                                         reg[:, 0:128], s_rec[:, qs % 2:qs % 2 + 1], None,
                                         ALU.mult))
                    vw()
                    v.sem_inc(S("CX"), 1)
            barrier(v, 3)
            if not P1C:
                return
            for ob in range(8):
                for ns in range(8):
                    tt = 8 * ob + ns
                    v.wait_ge(S("OP"), tt + 1)
                    v.wait_ge(S(f"XB{tt % 2}"), 16 * (tt // 2 + 1))
                    if tt >= 2:
                        v.wait_ge(S(f"X2W{tt % 2}"), 16 * (tt // 2))
                    v.tensor_tensor(x2st(tt % 2),
                                    ps[:, (tt % 4) * 512:(tt % 4 + 1) * 512],
                                    xblk(tt % 2), ALU.add).then_inc(S("X2S"), 1)
            barrier(v, 4)
            if not P2:
                return
            for ns in range(8):
                vw()
                for hlf in range(2):
                    v.wait_ge(S(f"X2L{(ns % 2) * 2 + hlf}"), 16 * (ns // 2 + 1))
                    for q in range(4):
                        vink(v.bn_stats(s_bnst[:, hlf * 24 + q * 6: hlf * 24 + (q + 1) * 6],
                                        xtile(ns % 2, hlf, q * 512, (q + 1) * 512)))
                vw()
                v.sem_inc(S("BN2"), 1)
                if ns >= 1:
                    v.wait_ge(S("R2"), ns)
                vw()
                vink(v.bn_aggr(s_bn[:], s_bnst[:, 0:48]))
                vw()
                v.sem_inc(S("BN2"), 1)
                v.wait_ge(S("R2"), ns + 1)
                if ns >= 2:
                    v.wait_ge(S(f"XT2{ns % 2}"), 16 * (ns // 2))
                vw()
                for hlf in range(2):
                    vink(v.tensor_scalar(xhbuf(ns % 2)[:, hlf * 2048:(hlf + 1) * 2048],
                                         xtile(ns % 2, hlf, 0, 2048), s_bn[:, 0:1], s_r[:],
                                         ALU.subtract, ALU.mult))
                    vw()
                    v.sem_inc(S("NM2"), 1)
            if not P2R:
                return
            v.wait_ge(S("LG"), 1)
            vink(v.tensor_copy(s_lgs[:], ps[:, 3072:3136]))
            for ns in range(8):
                vw()
                vink(v.max(s_l8[:, ns * 8:(ns + 1) * 8],
                           s_lgs[:, ns * 8:(ns + 1) * 8]))
                vw()
                vink(v.tensor_tensor(s_d[:, ns:ns + 1], s_l8[:, ns * 8 + 1:ns * 8 + 2],
                                     s_l8[:, ns * 8:ns * 8 + 1], ALU.subtract))
                vw()
                v.sem_inc(S("D"), 1)
            for ns in range(8):
                v.wait_ge(S("T"), ns + 1)
                vw()
                vink(v.tensor_scalar(s_sc1[:], s_t[:, ns:ns + 1], 1.0, None, ALU.add))
                vw()
                vink(v.reciprocal(s_w1[:], s_sc1[:]))
                vw()
                vink(v.tensor_tensor(s_w2[:], s_t[:, ns:ns + 1], s_w1[:], ALU.mult))
                lg = s_lgs[:, ns * 8:(ns + 1) * 8]
                vink(v.tensor_scalar(s_eqA[:], lg, s_l8[:, ns * 8:ns * 8 + 1], None, ALU.is_equal))
                vw()
                vink(v.tensor_scalar(s_eqB[:], lg, s_l8[:, ns * 8 + 1:ns * 8 + 2], None, ALU.is_equal))
                vink(v.tensor_scalar(s_comb[:, ns * 8:(ns + 1) * 8], s_eqA[:], s_w1[:], None, ALU.mult))
                vw()
                vink(v.scalar_tensor_tensor(s_comb2[:, ns * 8:(ns + 1) * 8], s_eqB[:], s_w2[:],
                                            s_comb[:, ns * 8:(ns + 1) * 8],
                                            ALU.mult, ALU.add))
                vw()
                v.sem_inc(S("CMB"), 1)
            if not P2H:
                return
            v.wait_ge(S("TT"), 8)
            vink(v.tensor_copy(s_combT[0:8, :], ps[0:8, 2048:3072]))
            vw()
            v.sem_inc(S("CT"), 1)
            if sub_ord < 2:
                return
            v.wait_ge(S("E2S"), 1)
            vink(v.tensor_copy(ce_sb[:], ps[:, 0:1024]))
            vw()
            v.sem_inc(S("CE"), 1)
            if not P2H or sub_ord < 3:
                return
            v.wait_ge(S("HM"), 1)
            vw()
            vink(v.tensor_tensor(hs_sb[:], ps[:, 1024:2048], ce_sb[:], ALU.mult))
            vw()
            v.sem_inc(S("HS"), 1)
            if not P2Y:
                return
            for ob in range(8):
                for ns in range(8):
                    tt = 8 * ob + ns
                    v.wait_ge(S("OP2"), tt + 1)
                    v.wait_ge(S(f"XB2{tt % 2}"), 16 * (tt // 2 + 1))
                    if tt >= 2:
                        v.wait_ge(S(f"YW{tt % 2}"), 16 * (tt // 2))
                    v.tensor_tensor(yst(tt % 2),
                                    ps[:, (tt % 4) * 512:(tt % 4 + 1) * 512],
                                    xblk(tt % 2), ALU.add).then_inc(S("YS"), 1)

        @blk.scalar
        def _(sc):
            _ac = [0]
            sc.wait_ge(S("CST"), 1)
            for nss in range(2):
                for hlf in range(2):
                    sc.dma_start(out=xtile(nss, hlf, 0, 2048),
                                 in_=x[nss * 128:(nss + 1) * 128, hlf * 2048:(hlf + 1) * 2048]
                                 ).then_inc(S(f"X{nss * 2 + hlf}"), 16)
            for ns in range(8):
                sc.wait_ge(S("BN"), 2 * ns + 2)
                if ns >= 1:
                    sc.wait_ge(S("NM"), 2 * ns)
                _ac[0] += 1
                sc.activation(s_lnt[:], s_bn[:, 1:2], AF.Ln, bias=cst[:, 0:1],
                              scale=1.0).then_inc(S("ACH"), 1)
                sc.wait_ge(S("ACH"), _ac[0])
                _ac[0] += 1
                sc.activation(s_r[:], s_lnt[:], AF.Exp, scale=-0.5).then_inc(S("ACH"), 1)
                sc.wait_ge(S("ACH"), _ac[0])
                sc.sem_inc(S("R"), 1)
                if ns + 2 < 8:
                    for hlf in range(2):
                        sc.wait_ge(S("NM"), 2 * ns + hlf + 1)
                        sc.dma_start(out=xtile(ns % 2, hlf, 0, 2048),
                                     in_=x[(ns + 2) * 128:(ns + 3) * 128,
                                           hlf * 2048:(hlf + 1) * 2048]
                                     ).then_inc(S(f"X{(ns % 2) * 2 + hlf}"), 16)
            barrier(sc, 1)
            barrier(sc, 2)
            if not P1B:
                return
            sc.wait_ge(S("CST"), 2)
            for h in range(NH):
                for kt in range(16):
                    g = 16 * h + kt
                    sc.wait_ge(S("SC"), g + 1)
                    if kt == 0 and h >= 2:
                        sc.wait_ge(S("AV"), 8 * (h - 1))
                    base = (g % 3) * 1024
                    sc.activation(ar1((h % 2) * 16384 + kt * 1024, (h % 2) * 16384 + (kt + 1) * 1024),
                                  ps[:, base:base + 1024], AF.Exp,
                                  bias=cst[:, 1:2], scale=SCL).then_inc(S("EX"), 1)
            barrier(sc, 3)
            if not P1C:
                return
            def _xsrc(tt):
                ob_, ns_ = tt // 8, tt % 8
                return x[ns_ * 128:(ns_ + 1) * 128, ob_ * 512:(ob_ + 1) * 512]
            sc.wait_ge(S("QK"), 96)
            for tt in range(2):
                sc.dma_start(out=xblk(tt % 2), in_=_xsrc(tt)).then_inc(S(f"XB{tt % 2}"), 16)
            for tt in range(64):
                ob_, ns_ = tt // 8, tt % 8
                sc.wait_ge(S("X2S"), tt + 1)
                sc.dma_start(out=x2_d[ns_ * 128:(ns_ + 1) * 128, ob_ * 512:(ob_ + 1) * 512],
                             in_=x2st(tt % 2)).then_inc(S(f"X2W{tt % 2}"), 16)
                if tt + 2 < 64:
                    sc.dma_start(out=xblk(tt % 2), in_=_xsrc(tt + 2)).then_inc(S(f"XB{tt % 2}"), 16)
            barrier(sc, 4)
            if not P2:
                return
            sc.wait_ge(S("X2W0"), 16 * 32)
            sc.wait_ge(S("X2W1"), 16 * 32)
            for nss in range(2):
                for hlf in range(2):
                    sc.dma_start(out=xtile(nss, hlf, 0, 2048),
                                 in_=x2_d[nss * 128:(nss + 1) * 128, hlf * 2048:(hlf + 1) * 2048]
                                 ).then_inc(S(f"X2L{nss * 2 + hlf}"), 16)
            for ns in range(8):
                sc.wait_ge(S("BN2"), 2 * ns + 2)
                if ns >= 1:
                    sc.wait_ge(S("NM2"), 2 * ns)
                _ac[0] += 1
                sc.activation(s_lnt[:], s_bn[:, 1:2], AF.Ln, bias=cst[:, 0:1],
                              scale=1.0).then_inc(S("ACH"), 1)
                sc.wait_ge(S("ACH"), _ac[0])
                _ac[0] += 1
                sc.activation(s_r[:], s_lnt[:], AF.Exp, scale=-0.5).then_inc(S("ACH"), 1)
                sc.wait_ge(S("ACH"), _ac[0])
                sc.sem_inc(S("R2"), 1)
                if ns + 2 < 8:
                    for hlf in range(2):
                        sc.wait_ge(S("NM2"), 2 * ns + hlf + 1)
                        sc.dma_start(out=xtile(ns % 2, hlf, 0, 2048),
                                     in_=x2_d[(ns + 2) * 128:(ns + 3) * 128,
                                              hlf * 2048:(hlf + 1) * 2048]
                                     ).then_inc(S(f"X2L{(ns % 2) * 2 + hlf}"), 16)
            if not P2R:
                return
            for ns in range(8):
                sc.wait_ge(S("D"), ns + 1)
                sc.activation(s_t[:, ns:ns + 1], s_d[:, ns:ns + 1], AF.Exp,
                              scale=1.0).then_inc(S("T"), 1)
            if not P2Y:
                return
            sc.wait_ge(S("NM2"), 16)
            def _x2src(tt):
                ob_, ns_ = tt // 8, tt % 8
                return x2_d[ns_ * 128:(ns_ + 1) * 128, ob_ * 512:(ob_ + 1) * 512]
            for tt in range(2):
                sc.dma_start(out=xblk(tt % 2), in_=_x2src(tt)).then_inc(S(f"XB2{tt % 2}"), 16)
            for tt in range(64):
                ob_, ns_ = tt // 8, tt % 8
                sc.wait_ge(S("YS"), tt + 1)
                sc.dma_start(out=y[ns_ * 128:(ns_ + 1) * 128, ob_ * 512:(ob_ + 1) * 512],
                             in_=yst(tt % 2)).then_inc(S(f"YW{tt % 2}"), 16)
                if tt + 2 < 64:
                    sc.dma_start(out=xblk(tt % 2), in_=_x2src(tt + 2)).then_inc(S(f"XB2{tt % 2}"), 16)
            sc.wait_ge(S("YW0"), 16 * 32)
            sc.wait_ge(S("YW1"), 16 * 32)

        @blk.sync
        def _(s):
            for ns in range(8):
                s.wait_ge(S("NM"), 2 * (ns + 1))
                s.dma_start(out=ar1_3d[:, :, ns * 128:(ns + 1) * 128],
                            in_=xhbuf(ns % 2)[:], transpose=True).then_inc(S(f"XT{ns % 2}"), 16)
            barrier(s, 1)
            if not P1A:
                return
            for i in range(N_PAN):
                for jj in range(2):
                    tt = 2 * i + jj
                    jcol = P_ORD[i] * 2 + jj
                    s.wait_ge(S("EV"), tt + 1)
                    dest = _qkv_dest(jcol)
                    if dest[0] == "q":
                        dst = qt_d[dest[1]:dest[1] + 128, :]
                    else:
                        dst = agin[dest[1] * 2048 + dest[2]: dest[1] * 2048 + dest[2] + 128, :]
                    s.dma_start(out=dst, in_=stage[:, tt % 2, :]).then_inc(S(f"STQ{tt % 2}"), 16)
            barrier(s, 2)
            if not P1B:
                return
            s.wait_ge(S("STQ0"), 16 * 48)
            s.wait_ge(S("STQ1"), 16 * 48)
            for h in range(NH):
                ch, hl = h // 8, h % 8
                s.wait_ge(S(f"KVL{ch}"), 16)
                if h >= 2:
                    s.wait_ge(S("SC"), 16 * (h - 1))
                    s.wait_ge(S("AV"), 8 * (h - 1))
                s.dma_start(out=KT[:, h % 2, 0:1024],
                            in_=kvloc[ch * 4096 + hl * 128: ch * 4096 + (hl + 1) * 128, :]).then_inc(S(f"ATL{h % 2}"), 16)
                s.dma_start(out=KT[:, h % 2, 1024:2048],
                            in_=kvloc[ch * 4096 + 2048 + hl * 128: ch * 4096 + 2048 + (hl + 1) * 128, :]).then_inc(S(f"ATL{h % 2}"), 16)
                s.dma_start(out=QT[:, h % 2, :],
                            in_=qt_d[h * 128:(h + 1) * 128, :]).then_inc(S(f"ATL{h % 2}"), 16)
                s.dma_start(out=VA[:, h % 2, 0:8, 0:128],
                            in_=kvloc[ch * 4096 + 1024 + hl * 128: ch * 4096 + 1024 + (hl + 1) * 128, :],
                            transpose=True).then_inc(S(f"VTL{h % 2}"), 16)
                s.dma_start(out=VA[:, h % 2, 8:16, 0:128],
                            in_=kvloc[ch * 4096 + 3072 + hl * 128: ch * 4096 + 3072 + (hl + 1) * 128, :],
                            transpose=True).then_inc(S(f"VTL{h % 2}"), 16)
            barrier(s, 3)
            if not P1C:
                return
            s.wait_ge(S("CX"), 256)
            for qs in range(8):
                s.dma_start(out=ar1_3d[:, :, qs * 128:(qs + 1) * 128],
                            in_=AR2[:, qs * 4096:(qs + 1) * 4096], transpose=True
                            ).then_inc(S("TP"), 16)
            barrier(s, 4)
            if not P2:
                return
            s.wait_ge(S("OP"), 64)
            for ns in range(8):
                s.wait_ge(S("NM2"), 2 * (ns + 1))
                s.dma_start(out=ar1_3d[:, :, ns * 128:(ns + 1) * 128],
                            in_=xhbuf(ns % 2)[:], transpose=True).then_inc(S(f"XT2{ns % 2}"), 16)

        @blk.gpsimd
        def _(g):
            g.dma_start(out=ci_sb[:], in_=cinfo.ap()).then_inc(S("SET"), 16)
            g.dma_start(out=bqkv_sb[:], in_=bqkv.ap()).then_inc(S("SET"), 16)
            g.dma_start(out=wrt_sb[:], in_=wrt.ap()).then_inc(S("SET"), 16)
            g.dma_start(out=a2_sb[:], in_=a2.ap()).then_inc(S("SET"), 16)
            g.dma_start(out=b2_sb[:], in_=b2.ap()).then_inc(S("SET"), 16)
            g.dma_start(out=e2_sb[:], in_=e2.ap()).then_inc(S("SET"), 16)
            g.dma_start(out=idf_sb[:], in_=idf.ap()).then_inc(S("SET"), 16)
            g.wait_ge(S("SET"), SETTOT)
            g.reg_load(rb, ci_sb[0:1, 0:1])
            barrier(g, 1)
            if not P1A:
                return
            wq_r = wqkv.ap().rearrange("(cc p) c -> p cc c", p=128)
            g.wait_ge(S("NM"), 16)
            for i in range(N_PAN):
                p = P_ORD[i]
                if i >= 5:
                    g.wait_ge(S("QK"), 2 * (i - 4))
                g.dma_start(out=wslot3d(i % 5),
                            in_=wq_r[:, :, p * 256:(p + 1) * 256]).then_inc(S(f"WP{i % 5}"), 16)
                if i >= 12 and (i - 12) % 8 == 0 and (i - 12) // 8 < NCH:
                    ch = (i - 12) // 8
                    g.wait_ge(S("STQ0"), 16 * (8 * ch + 8))
                    g.wait_ge(S("STQ1"), 16 * (8 * ch + 8))
                    g.collective_compute(
                        "AllGather", ALU.bypass,
                        replica_groups=[[0, 1, 2, 3], [4, 5, 6, 7]],
                        ins=[agin[ch * 2048:(ch + 1) * 2048, :].opt()],
                        outs=[agout[ch * 8192:(ch + 1) * 8192, :].opt()],
                    ).then_inc(sCC[ch], 1)
            for ch in range(NCH):
                g.wait_ge(sCC[ch], 1)
                g.reg_mov(r2, ch * 8192 * 1024)
                g.reg_add(r2, r2, rb)
                g.dma_start(out=kvloc[ch * 4096:(ch + 1) * 4096, :],
                            in_=bass.AP(agout, r2, [[1024, 4096], [1, 1024]])
                            ).then_inc(S(f"KVL{ch}"), 16)
            barrier(g, 2)
            barrier(g, 3)
            if not P1C:
                return
            wo_r = wo.ap().rearrange("(cc p) o -> p cc o", p=128)
            g.wait_ge(S("TP"), 16 * 8)
            for ob in range(8):
                if ob >= 2:
                    g.wait_ge(S("OP"), 8 * (ob - 1))
                g.dma_start(out=AR2[:, (ob % 2) * 16384:(ob % 2 + 1) * 16384]
                            .rearrange("p (cc n) -> p cc n", n=512),
                            in_=wo_r[:, :, ob * 512:(ob + 1) * 512]).then_inc(S(f"WO{ob % 2}"), 16)
                g.dma_start(out=obp[0:1, ob % 2, :],
                            in_=obr.ap()[0:1, ob * 512:(ob + 1) * 512]).then_inc(S(f"WO{ob % 2}"), 16)
            barrier(g, 4)
            if not P2Y:
                return
            wb_r = wbt.ap().rearrange("(cc p) o -> p cc o", p=128)
            g.wait_ge(S("OP"), 64)
            g.wait_ge(S("NM2"), 16)
            for ob in range(8):
                if ob >= 2:
                    g.wait_ge(S("OP2"), 8 * (ob - 1))
                g.dma_start(out=AR2[:, (ob % 2) * 16384:(ob % 2 + 1) * 16384]
                            .rearrange("p (cc n) -> p cc n", n=512),
                            in_=wb_r[:, :, ob * 512:(ob + 1) * 512]).then_inc(S(f"WB{ob % 2}"), 16)
                g.dma_start(out=bbp[0:1, ob % 2, :],
                            in_=bbr.ap()[0:1, ob * 512:(ob + 1) * 512]).then_inc(S(f"WB{ob % 2}"), 16)

    return nc


def kernel(**inputs):
    X = np.ascontiguousarray(np.asarray(inputs["L0_attn_x"], dtype=np.float32))
    wqkv = np.ascontiguousarray(np.asarray(inputs["L0_attn_attn_qkvw"], dtype=np.float32))
    qkvb = np.asarray(inputs["L0_attn_attn_qkvb"], dtype=np.float32)
    wo = np.ascontiguousarray(np.asarray(inputs["L0_attn_attn_ow"], dtype=np.float32))
    ob_ = np.asarray(inputs["L0_attn_attn_ob"], dtype=np.float32)
    wb = np.asarray(inputs["L0_ffn_W_base"], dtype=np.float32)
    bb = np.asarray(inputs["L0_ffn_b_base"], dtype=np.float32)
    wr = np.asarray(inputs["L0_ffn_W_router"], dtype=np.float32)
    A = np.asarray(inputs["L0_ffn_A_stack"], dtype=np.float32)
    B = np.asarray(inputs["L0_ffn_B_stack"], dtype=np.float32)

    bqkv = np.ascontiguousarray(qkvb.reshape(96, 128).T)
    obr = np.ascontiguousarray(ob_.reshape(1, 4096))
    wbt = np.ascontiguousarray(wb.T)
    bbr = np.ascontiguousarray(bb.reshape(1, 4096))
    wrt = np.ascontiguousarray(wr.T.reshape(32, 128, 8).transpose(1, 0, 2))
    a2 = np.ascontiguousarray(A.transpose(2, 0, 1).reshape(32, 128, 128).transpose(1, 0, 2))
    b2 = np.ascontiguousarray(B.transpose(0, 2, 1).reshape(128, 4096))
    e2 = np.zeros((8, 128), np.float32)
    for e in range(8):
        e2[e, e * 16:(e + 1) * 16] = 2.0
    idf = np.eye(128, dtype=np.float32)

    nc = build_nc()
    in_maps = []
    for c in range(8):
        b, h = c // 2, c % 2
        ci = np.zeros((1, 4), np.int32)
        ci[0, 0] = ((2 * b) % 4) * 2048 * 1024
        in_maps.append({
            "x": np.ascontiguousarray(X[b, h * 1024:(h + 1) * 1024, :]),
            "wqkv": wqkv, "bqkv": bqkv, "wo": wo, "obr": obr,
            "wbt": wbt, "bbr": bbr, "wrt": wrt, "a2": a2, "b2": b2,
            "e2": e2, "idf": idf, "cinfo": ci,
        })
    res = run_bass_kernel_spmd(nc, in_maps, core_ids=list(range(8)))
    Y = np.empty((4, 2048, 4096), np.float32)
    for c in range(8):
        b, h = c // 2, c % 2
        Y[b, h * 1024:(h + 1) * 1024, :] = np.asarray(res.results[c]["y"])
    return Y
